# version 14
# baseline (speedup 1.0000x reference)
import contextlib

import numpy as np
import concourse.bass as bass
import concourse.mybir as mybir
from concourse.bass_utils import run_bass_kernel_spmd

F32 = mybir.dt.float32
F32R = mybir.dt.float32r
BF16 = mybir.dt.bfloat16
I32 = mybir.dt.int32
ALU = mybir.AluOpType
ACTF = mybir.ActivationFunctionType
AX = mybir.AxisListType

P = 128
S = 4096
D = 1024
DH = 128
FF = 4096
NSC = S // P
NQB = 8
NHC = D // P
NOC = FF // P
NOB = 8
SBLK = 512
NSB = S // SBLK
SCQ = 4
ATT_SCALE = float(1.0 / np.sqrt(128.0))
EPS = 1e-5
INV127 = float(1.0 / 127.0)
INV127_HI = float(np.float32(1.0 / 127.0))
INV127_LO = float(np.float32(1.0 / 127.0 - np.float64(np.float32(1.0 / 127.0))))
INVD = float(1.0 / D)
WMEAN_N = float(1.0 / (D * FF))
C0GU = float(np.float32(np.sqrt(2.0 / (np.pi * 1024.0))))
C0D = float(np.float32(np.sqrt(2.0 / (np.pi * 4096.0))))



class _AD:

    _PASS = {"wait_ge", "dma_start", "drain", "wait_op"}

    def __init__(self, e):
        self._e = e

    def __getattr__(self, n):
        f = getattr(self._e, n)
        if n in self._PASS or not callable(f):
            return f
        def g(*a, **k):
            f(*a, **k)
            return self._e.drain()
        return g


class Sems:
    def __init__(self, nc, names):
        self._h = {n: nc.alloc_semaphore(name=n) for n in names}

    def __getitem__(self, n):
        return self._h[n]


SEM_NAMES = [
    "idn", "ldw", "ldx", "ldwb", "ldo0", "ldo1", "wst0", "wst1",
    "x1sp0", "x1sp1", "xqsp0", "xqsp1", "wqsp0", "wqsp1",
    "ldxq0", "ldxq1", "ldgu0", "ldgu1", "ldhq0", "ldhq1", "ldhq2", "ldhq3",
    "ldx10", "ldx11",
    "outd0", "outd1", "qkv", "qkve", "vtr", "vtre",
    "ss", "pexp", "avrs", "hev", "rdone", "rcolT", "rcole", "y1",
    "ldo", "z1", "sqrt1", "x1q", "xqd", "xtr", "xtre", "xqspill", "x1spill",
    "wst", "wabs", "wsum", "wscal", "wbc", "wsc", "wqz", "wqspill",
    "ldxq", "ldgu", "gu", "gpre", "silu", "mid", "rnd", "htr", "htre", "hqspill",
    "ldhq", "ldwd", "ldx1", "dwn", "sqrt2", "z2st", "outst", "outv",
]


def build_program():
    nc = bass.Bass("TRN2", target_bir_lowering=False, debug=False)

    d = {}
    d["xT"] = nc.dram_tensor("xT", [D, S], F32R, kind="ExternalInput").ap()
    d["x"] = nc.dram_tensor("x", [S, D], F32, kind="ExternalInput").ap()
    d["w_q"] = nc.dram_tensor("w_q", [D, DH], F32, kind="ExternalInput").ap()
    d["w_k"] = nc.dram_tensor("w_k", [D, DH], F32, kind="ExternalInput").ap()
    d["w_v"] = nc.dram_tensor("w_v", [D, DH], F32, kind="ExternalInput").ap()
    d["w_o"] = nc.dram_tensor("w_o", [DH, D], F32, kind="ExternalInput").ap()
    d["mask"] = nc.dram_tensor("mask", [DH, 1], I32, kind="ExternalInput").ap()
    d["wgT"] = nc.dram_tensor("wgT", [D, FF], F32, kind="ExternalInput").ap()
    d["wuT"] = nc.dram_tensor("wuT", [D, FF], F32, kind="ExternalInput").ap()
    d["wdT"] = nc.dram_tensor("wdT", [FF, D], F32, kind="ExternalInput").ap()
    d["out"] = nc.dram_tensor("out", [S, D], F32, kind="ExternalOutput").ap()
    d["x1s"] = nc.dram_tensor("x1s", [S, D], F32, kind="Internal").ap()
    d["xqTs"] = nc.dram_tensor("xqTs", [D, S], BF16, kind="Internal").ap()
    d["wgqs"] = nc.dram_tensor("wgqs", [D, FF], BF16, kind="Internal").ap()
    d["wuqs"] = nc.dram_tensor("wuqs", [D, FF], BF16, kind="Internal").ap()
    d["wdqs"] = nc.dram_tensor("wdqs", [FF, D], BF16, kind="Internal").ap()
    d["hqTs"] = nc.dram_tensor("hqTs", [FF, S], BF16, kind="Internal").ap()

    sm = Sems(nc, SEM_NAMES)

    with (
        nc.sbuf_tensor("ones_b", [P, 1], BF16) as ones_b,
        nc.sbuf_tensor("onesf", [P, 1], F32) as onesf,
        nc.sbuf_tensor("ones1f", [1, P], F32) as ones1f,
        nc.sbuf_tensor("identb", [P, P], BF16) as identb,
        nc.sbuf_tensor("identf", [P, P], F32) as identf,
        nc.sbuf_tensor("wsv", [P, 8], F32) as wsv,
        nc.sbuf_tensor("yscale", [P, NSC], F32) as yscale,
        nc.sbuf_tensor("aqs", [P, NSC], F32) as aqs,
        nc.sbuf_tensor("srow", [1, 8], F32) as srow,
        nc.sbuf_tensor("stats", [P, 16], F32) as stats,
        nc.sbuf_tensor("wstf", [P, 2, 1024], F32) as wstf,
        nc.sbuf_tensor("wacc", [P, 4], F32) as wacc,
        nc.sbuf_tensor("wpar", [P, 3, 32], F32) as wpar,
        nc.sbuf_tensor("crow", [1, 4], F32) as crow,
        nc.sbuf_tensor("wqzb", [P, 2, 1024], BF16) as wqzb,
        nc.sbuf_tensor("wu_", [P, 1024], F32) as wu_,
        nc.sbuf_tensor("wc2", [P, 1024], F32) as wc2,
        nc.sbuf_tensor("wc1", [P, 1024], F32) as wc1,
    ):
        pers = dict(ones_b=ones_b, onesf=onesf, ones1f=ones1f, identb=identb,
                    identf=identf, wsv=wsv, yscale=yscale, aqs=aqs, srow=srow,
                    stats=stats, wstf=wstf, wacc=wacc, wpar=wpar, crow=crow,
                    wqzb=wqzb, wu_=wu_, wc2=wc2, wc1=wc1)
        _phase_AB(nc, sm, d, pers)
        _phase_C1(nc, sm, d, pers)
        _phase_C2(nc, sm, d, pers)
    return nc


def _phase_AB(nc, sm, d, pers):
    ones_b = pers["ones_b"]; onesf = pers["onesf"]; ones1f = pers["ones1f"]
    identb = pers["identb"]; identf = pers["identf"]
    wsv = pers["wsv"]; srow = pers["srow"]; stats = pers["stats"]
    aqs = pers["aqs"]
    wstf = pers["wstf"]; wacc = pers["wacc"]; wpar = pers["wpar"]
    crow = pers["crow"]; wqzb = pers["wqzb"]; wu_ = pers["wu_"]
    wc2 = pers["wc2"]; wc1 = pers["wc1"]

    wplan = [(d["wgT"], d["wgqs"], NHC, 4, 1024, 0),
             (d["wuT"], d["wuqs"], NHC, 4, 1024, 1),
             (d["wdT"], d["wdqs"], NOC, 1, 1024, 2)]
    n_wch = sum(r * c for _, _, r, c, _, _ in wplan)
    wsched = []
    for c in range(4):
        for src, wi in ((d["wgT"], 0), (d["wuT"], 1)):
            for r in range(NHC):
                wsched.append((src, r, c, 1024, wi))
    for r in range(NOC):
        wsched.append((d["wdT"], r, 0, 1024, 2))
    wdsts = [d["wgqs"], d["wuqs"], d["wdqs"]]
    _cnt = [0, 0, 0]
    wseq = []
    for src, r, c, n, wi in wsched:
        wseq.append(_cnt[wi])
        _cnt[wi] += 1

    es = contextlib.ExitStack()
    def SB(name, shape, dt):
        return es.enter_context(nc.sbuf_tensor(name, shape, dt))
    def PS(name, shape, dt):
        return es.enter_context(nc.psum_tensor(name, shape, dt))
    with es:
        xts = SB("xts", [P, 2, NHC, 512], F32R)
        vt = SB("vt", [P, S], F32R)
        qt = SB("qt", [P, S], F32R)
        kt = SB("kt", [P, S], F32R)
        vtok = SB("vtok", [P, NSC, DH], F32R)
        wqkvb = SB("wqkvb", [P, NHC, 3 * DH], F32R)
        wof = SB("wof", [P, D], F32)
        wob = SB("wob", [P, D], F32R)
        maskf = SB("maskf", [P, 1], F32)
        pbuf = SB("pbuf", [P, 2, 512], F32R)
        htb = SB("htb", [P, 512], F32R)
        rsb = SB("rsb", [1, 512], F32)
        rinv = SB("rinv", [1, 512], F32)
        rcol = SB("rcol", [P, 4], F32)
        xo = SB("xo", [P, 2, D], F32)
        z1t = SB("z1t", [P, D], F32)
        tmp1 = SB("tmp1", [P, D], F32)
        x1t = SB("x1t", [P, 2, D], F32)
        tiq = SB("tiq", [P, D], I32)
        c1q = SB("c1q", [P, D], F32)
        xqdq = SB("xqdq", [P, 2, D], BF16)
        xqs = SB("xqs", [P, NHC, 512], BF16)
        ps_s = PS("ps_s", [P, 2, 512], F32)
        ps_h = PS("ps_h", [P, 512], F32)
        ps_r = PS("ps_r", [1, 512], F32)
        ps_y1 = PS("ps_y1", [P, 2, 512], F32)
        ps_vt = PS("ps_vt", [P, 512], F32R)
        ps_tr = PS("ps_tr", [P, 1024], BF16)
        identr = SB("identr", [P, P], F32R)
        ones_r = SB("ones_r", [P, 1], F32R)
        whm = SB("whm", [P, 8], F32)
        pv32 = ps_vt[:].bitcast(F32)
        blk = es.enter_context(nc.Block())
        @blk.sync
        def _(sy):
            sy.dma_start(out=wof[:], in_=d["w_o"]).then_inc(sm["ldw"], 16)
            for ch, (src, r, c, n, wi) in enumerate(wsched):
                if ch >= 2:
                    sy.wait_ge(sm["wabs"], ch - 1)
                sy.dma_start(
                    out=wstf[:, ch % 2, 0:n],
                    in_=src[r * P:(r + 1) * P, c * 1024:c * 1024 + n]
                ).then_inc(sm["wst0" if ch % 2 == 0 else "wst1"], 16)
            sy.wait_ge(sm["wsc"], 1)
            for ch, (src, r, c, n, wi) in enumerate(wsched):
                if ch >= 2:
                    sy.wait_ge(sm["wqz"], ch - 1)
                sy.dma_start(
                    out=wstf[:, ch % 2, 0:n],
                    in_=src[r * P:(r + 1) * P, c * 1024:c * 1024 + n]
                ).then_inc(sm["wst0" if ch % 2 == 0 else "wst1"], 16)

        @blk.gpsimd
        def _(gp):
            gp = _AD(gp)
            gp.wait_ge(sm["idn"], 1)
            gp.affine_select(out=identb[:], in_=identb[:],
                             compare_op=ALU.not_equal, fill=1.0, base=0,
                             pattern=[[-1, P]], channel_multiplier=1)
            gp.affine_select(out=identf[:], in_=identf[:],
                             compare_op=ALU.not_equal, fill=1.0, base=0,
                             pattern=[[-1, P]], channel_multiplier=1)
            gp.drain().then_inc(sm["idn"], 1)
            for i, w in enumerate((d["w_q"], d["w_k"], d["w_v"])):
                for h in range(NHC):
                    gp.dma_start(out=wqkvb[:, h, i * DH:(i + 1) * DH],
                                 in_=w[h * P:(h + 1) * P, :]
                                 ).then_inc(sm["ldwb"], 16)
            gp.dma_start(out=maskf[:], in_=d["mask"]).then_inc(sm["ldwb"], 16)
            for sb in range(NQB):
                if sb >= 2:
                    gp.wait_ge(sm["qkv"], 3 * (sb - 1))
                gp.dma_start(out=xts[:, sb % 2, :, :],
                             in_=d["xT"].rearrange("(a p) s -> p a s", p=P)[
                                 :, :, sb * 512:(sb + 1) * 512]
                             ).then_inc(sm["ldx"], 16)
                gp.wait_ge(sm["ldx"], 16 * (sb + 1))
            def _xo(sc):
                if sc >= NSC:
                    return
                if sc >= 2:
                    gp.wait_ge(sm["z1"], 2 * (sc - 1))
                gp.dma_start(out=xo[:, sc % 2, :],
                             in_=d["x"][sc * P:(sc + 1) * P, :]
                             ).then_inc(sm["ldo0" if sc % 2 == 0 else "ldo1"], 16)
            _xo(0)
            _xo(1)
            _wsp = [0]
            def wsp_chunks(k):
                while k > 0 and _wsp[0] < len(wsched):
                    ch = _wsp[0]
                    src, r, c, n, wi = wsched[ch]
                    dst = wdsts[wi]
                    gp.wait_ge(sm["wqz"], ch + 1)
                    gp.dma_start(
                        out=dst[r * P:(r + 1) * P, c * 1024:c * 1024 + n],
                        in_=wqzb[:, ch % 2, 0:n]
                    ).then_inc(sm["wqsp0" if ch % 2 == 0 else "wqsp1"], 16)
                    _wsp[0] += 1
                    k -= 1
            for sc in range(NSC):
                gp.wait_ge(sm["x1q"], sc + 1)
                gp.dma_start(out=d["x1s"][sc * P:(sc + 1) * P, :],
                             in_=x1t[:, sc % 2, :]
                             ).then_inc(sm["x1sp0" if sc % 2 == 0 else "x1sp1"], 16)
                if sc % 4 == 3:
                    qb = sc // 4
                    gp.wait_ge(sm["xtre"], (qb + 1) * 4)
                    for ic in range(NHC):
                        gp.dma_start(
                            out=d["xqTs"][ic * P:(ic + 1) * P,
                                          qb * 512:(qb + 1) * 512],
                            in_=xqs[:, ic, :]
                        ).then_inc(sm["xqsp0" if qb % 2 == 0 else "xqsp1"], 16)
                    gp.wait_ge(sm["xqsp0" if qb % 2 == 0 else "xqsp1"],
                               16 * NHC * (qb // 2 + 1))
                _xo(sc + 2)
                if sc >= 16:
                    wsp_chunks(6)
            wsp_chunks(len(wsched))

        @blk.vector
        def _(v):
            v = _AD(v)
            v.memset(identb[:], 0.0)
            v.memset(identf[:], 0.0)
            v.memset(ones_b[:], 1.0)
            v.memset(onesf[:], 1.0)
            v.memset(ones1f[:], 1.0)
            v.memset(wacc[:], 0.0)
            v.drain().then_inc(sm["idn"], 1)
            v.wait_ge(sm["idn"], 2)
            v.tensor_copy(identr[:], identf[:])
            v.tensor_copy(ones_r[:], onesf[:])
            v.drain().then_inc(sm["idn"], 1)
            v.wait_ge(sm["ldw"], 16)
            v.wait_ge(sm["ldwb"], 16 * 25)
            v.tensor_scalar(wof[:], wof[:], maskf[:], None, ALU.mult)
            v.tensor_copy(wob[:], wof[:])
            v.memset(crow[0:1, 0:1], C0GU)
            v.memset(crow[0:1, 1:2], C0GU)
            v.memset(crow[0:1, 2:3], C0D)
            v.memset(crow[0:1, 3:4], 0.0)
            _w1 = [0]
            def w1_chunks(k):
                while k > 0 and _w1[0] < len(wsched):
                    ch = _w1[0]
                    src, r, c, n, wi = wsched[ch]
                    c0 = C0GU if wi < 2 else C0D
                    v.wait_ge(sm["wst0" if ch % 2 == 0 else "wst1"],
                              16 * (ch // 2 + 1))
                    v.reduce_sum(stats[:, 14:15], wstf[:, ch % 2, 0:n],
                                 AX.X, apply_absolute_value=True)
                    wch = wseq[ch]
                    v.tensor_scalar(wpar[:, wi, wch:wch + 1],
                                    stats[:, 14:15], 1.0, -float(n) * c0,
                                    ALU.mult, ALU.add
                                    ).then_inc(sm["wabs"], 1)
                    _w1[0] += 1
                    k -= 1
            _w2 = [0]
            def w2_chunks(k):
                while k > 0 and _w2[0] < len(wsched):
                    ch = _w2[0]
                    src, r, c, n, wi = wsched[ch]
                    v.wait_ge(sm["wst0" if ch % 2 == 0 else "wst1"],
                              16 * (len(wsched) // 2 + ch // 2 + 1))
                    wch = wstf[:, ch % 2, 0:n]
                    v.tensor_scalar(wc1[:, 0:n], wch, whm[:, wi:wi + 1],
                                    None, ALU.is_gt)
                    v.tensor_scalar(wc2[:, 0:n], wch, whm[:, 4 + wi:5 + wi],
                                    None, ALU.is_lt)
                    if ch >= 2:
                        v.wait_ge(sm["wqsp0" if ch % 2 == 0 else "wqsp1"],
                                  16 * (ch // 2))
                    v.tensor_tensor(wqzb[:, ch % 2, 0:n], wc1[:, 0:n],
                                    wc2[:, 0:n], ALU.subtract
                                    ).then_inc(sm["wqz"], 1)
                    _w2[0] += 1
                    k -= 1
            for qb in range(NQB):
                if qb <= 2:
                    w1_chunks(32)
                if qb == 2:
                    for wi in range(3):
                        v.reduce_sum(wacc[:, wi:wi + 1], wpar[:, wi, :], AX.X)
                    v.drain().then_inc(sm["wabs"], 1)
                if qb == 3:
                    v.wait_ge(sm["wsum"], 1)
                    v.tensor_copy(srow[0:1, 0:4], pv32[0:1, 0:4])
                    v.scalar_tensor_tensor(srow[0:1, 0:4], srow[0:1, 0:4],
                                           WMEAN_N, crow[0:1, 0:4],
                                           ALU.mult, ALU.add)
                    v.tensor_scalar(srow[0:1, 0:4], srow[0:1, 0:4], 1.0, 1e-5,
                                    ALU.mult, ALU.max)
                    v.reciprocal(srow[0:1, 4:8], srow[0:1, 0:4]
                                 ).then_inc(sm["wscal"], 1)
                if qb == 4:
                    v.wait_ge(sm["wbc"], 1)
                    v.tensor_copy(wsv[:], pv32[:, 4:12]).then_inc(sm["wsc"], 1)
                    v.tensor_scalar(whm[:, 0:4], wsv[:, 0:4], 0.5, None,
                                    ALU.mult)
                    v.tensor_scalar(whm[:, 4:8], wsv[:, 0:4], -0.5, None,
                                    ALU.mult)
                v.wait_ge(sm["avrs"], (qb + 1) * 32)
                v.tensor_copy(rsb[:], ps_r[:])
                v.reciprocal(rinv[:], rsb[:]).then_inc(sm["rdone"], 1)
                v.wait_ge(sm["rcolT"], qb + 1)
                v.tensor_copy(rcol[:], ps_y1[:, 0, 0:4]).then_inc(sm["rcole"], 1)
                for j in range(4):
                    sc = qb * 4 + j
                    v.wait_ge(sm["y1"], sc + 1)
                    v.wait_ge(sm["ldo0" if sc % 2 == 0 else "ldo1"],
                              16 * (sc // 2 + 1))
                    v.scalar_tensor_tensor(z1t[:, 0:512], ps_y1[:, 0, :],
                                           rcol[:, j:j + 1],
                                           xo[:, sc % 2, 0:512],
                                           ALU.mult, ALU.add)
                    v.scalar_tensor_tensor(z1t[:, 512:1024], ps_y1[:, 1, :],
                                           rcol[:, j:j + 1],
                                           xo[:, sc % 2, 512:1024],
                                           ALU.mult, ALU.add
                                           ).then_inc(sm["z1"], 1)
                    v.reduce_sum(stats[:, 0:1], z1t[:], AX.X)
                    v.tensor_tensor(tmp1[:], z1t[:], z1t[:], ALU.mult)
                    v.reduce_sum(stats[:, 1:2], tmp1[:], AX.X)
                    v.tensor_scalar(stats[:, 2:3], stats[:, 0:1], INVD, None,
                                    ALU.mult)
                    v.tensor_tensor(stats[:, 3:4], stats[:, 2:3],
                                    stats[:, 2:3], ALU.mult)
                    v.scalar_tensor_tensor(stats[:, 4:5], stats[:, 1:2], INVD,
                                           stats[:, 3:4], ALU.mult,
                                           ALU.subtract)
                    v.tensor_scalar(stats[:, 4:5], stats[:, 4:5], 1.0, EPS,
                                    ALU.mult, ALU.add).then_inc(sm["z1"], 1)
                    v.wait_ge(sm["sqrt1"], sc + 1)
                    v.reciprocal(stats[:, 6:7], stats[:, 5:6])
                    for _i in range(2):
                        v.tensor_tensor(stats[:, 7:8], stats[:, 6:7],
                                        stats[:, 6:7], ALU.mult)
                        v.tensor_tensor(stats[:, 7:8], stats[:, 7:8],
                                        stats[:, 4:5], ALU.mult)
                        v.tensor_scalar(stats[:, 7:8], stats[:, 7:8], -0.5,
                                        1.5, ALU.mult, ALU.add)
                        v.tensor_tensor(stats[:, 6:7], stats[:, 6:7],
                                        stats[:, 7:8], ALU.mult)
                    if sc >= 2:
                        v.wait_ge(sm["x1sp0" if sc % 2 == 0 else "x1sp1"],
                                  16 * (sc // 2))
                    v.tensor_scalar(x1t[:, sc % 2, :], z1t[:], stats[:, 2:3],
                                    stats[:, 6:7], ALU.subtract, ALU.mult
                                    ).then_inc(sm["x1q"], 1)
                    v.reduce_max(stats[:, 8:9], x1t[:, sc % 2, :], AX.X,
                                 apply_absolute_value=True)
                    v.tensor_scalar(stats[:, 9:10], stats[:, 8:9], 1e-5, None,
                                    ALU.max)
                    v.reciprocal(stats[:, 10:11], stats[:, 9:10])
                    v.tensor_scalar(stats[:, 11:12], stats[:, 10:11], 127.0,
                                    None, ALU.mult)
                    v.tensor_scalar(aqs[:, sc:sc + 1], stats[:, 9:10],
                                    INV127_LO, None, ALU.mult)
                    v.scalar_tensor_tensor(aqs[:, sc:sc + 1], stats[:, 9:10],
                                           INV127_HI, aqs[:, sc:sc + 1],
                                           ALU.mult, ALU.add)
                    v.tensor_scalar(tmp1[:], x1t[:, sc % 2, :],
                                    stats[:, 11:12], 128.0, ALU.mult, ALU.add)
                    v.tensor_copy(tiq[:], tmp1[:])
                    v.scalar_tensor_tensor(c1q[:], tmp1[:], -0.5, tiq[:],
                                           ALU.add, ALU.is_ge)
                    if sc >= 2:
                        v.wait_ge(sm["xtr"], sc - 1)
                    v.scalar_tensor_tensor(xqdq[:, sc % 2, :], tiq[:], -128.0,
                                           c1q[:], ALU.add, ALU.add
                                           ).then_inc(sm["xqd"], 1)
                    if sc >= 16:
                        w2_chunks(6)
            w2_chunks(len(wsched))

        @blk.scalar
        def _(s):
            s = _AD(s)
            def _xtre_head(s, sc):
                j = sc % 4
                s.wait_ge(sm["xtr"], sc + 1)
                if sc >= 4 and j == 0:
                    qbp = sc // 4 - 1
                    s.wait_ge(sm["xqsp0" if qbp % 2 == 0 else "xqsp1"],
                              16 * NHC * (qbp // 2 + 1))
                s.copy(xqs[:, :, j * 128:(j + 1) * 128],
                       ps_tr[:].rearrange("p (a b) -> p a b", a=8)
                       ).then_inc(sm["xtre"], 1)
            for sb in range(NQB):
                s.wait_ge(sm["qkv"], (sb + 1) * 3)
                s.copy(qt[:, sb * 512:(sb + 1) * 512], ps_s[:, 0, :])
                s.copy(kt[:, sb * 512:(sb + 1) * 512], ps_s[:, 1, :])
                s.copy(vt[:, sb * 512:(sb + 1) * 512], ps_h[:]
                       ).then_inc(sm["qkve"], 1)
            for gidx in range(8):
                s.wait_ge(sm["vtr"], gidx + 1)
                s.copy(vtok[:, gidx * 4:(gidx + 1) * 4, :],
                       ps_vt[:].rearrange("p (a b) -> p a b", a=4)
                       ).then_inc(sm["vtre"], 1)
            for qb in range(NQB):
                base = qb * 32
                if qb >= 1:
                    for scp in (qb * 4 - 2, qb * 4 - 1):
                        _xtre_head(s, scp)
                for kc in range(32):
                    s.wait_ge(sm["ss"], base + kc + 1)
                    if kc >= 2:
                        s.wait_ge(sm["avrs"], base + kc - 1)
                    s.activation(pbuf[:, kc % 2, :], ps_s[:, kc % 2, :],
                                 ACTF.Exp, bias=0.0, scale=ATT_SCALE
                                 ).then_inc(sm["pexp"], 1)
                s.wait_ge(sm["avrs"], (qb + 1) * 32)
                s.copy(htb[:], ps_h[:]).then_inc(sm["hev"], 1)
                def _xtre(s, sc):
                    j = sc % 4
                    s.wait_ge(sm["xtr"], sc + 1)
                    if sc >= 4 and j == 0:
                        qbp = sc // 4 - 1
                        s.wait_ge(sm["xqsp0" if qbp % 2 == 0 else "xqsp1"],
                                  16 * NHC * (qbp // 2 + 1))
                    s.copy(xqs[:, :, j * 128:(j + 1) * 128],
                           ps_tr[:].rearrange("p (a b) -> p a b", a=8)
                           ).then_inc(sm["xtre"], 1)
                for j in range(4):
                    sc = qb * 4 + j
                    s.wait_ge(sm["z1"], 2 * (sc + 1))
                    s.activation(stats[:, 5:6], stats[:, 4:5], ACTF.Sqrt,
                                 bias=0.0, scale=1.0).then_inc(sm["sqrt1"], 1)
                    if j >= 2:
                        _xtre(s, sc - 2)
                if qb == NQB - 1:
                    _xtre(s, qb * 4 + 2)
                    _xtre(s, qb * 4 + 3)

        @blk.tensor
        def _(t):
            t.wait_ge(sm["idn"], 3)
            t.wait_ge(sm["ldwb"], 16 * 25)
            for sb in range(NQB):
                t.wait_ge(sm["ldx"], 16 * (sb + 1))
                if sb >= 1:
                    t.wait_ge(sm["qkve"], sb)
                for h in range(NHC):
                    t.matmul(ps_s[:, 0, :], wqkvb[:, h, 0:DH],
                             xts[:, sb % 2, h, :],
                             start=(h == 0), stop=(h == NHC - 1))
                t.drain().then_inc(sm["qkv"], 1)
                for h in range(NHC):
                    t.matmul(ps_s[:, 1, :], wqkvb[:, h, DH:2 * DH],
                             xts[:, sb % 2, h, :],
                             start=(h == 0), stop=(h == NHC - 1))
                t.drain().then_inc(sm["qkv"], 1)
                for h in range(NHC):
                    t.matmul(ps_h[:], wqkvb[:, h, 2 * DH:3 * DH],
                             xts[:, sb % 2, h, :],
                             start=(h == 0), stop=(h == NHC - 1))
                t.drain().then_inc(sm["qkv"], 1)
            for gidx in range(8):
                t.wait_ge(sm["qkve"], gidx + 1)
                if gidx >= 1:
                    t.wait_ge(sm["vtre"], gidx)
                for k in range(4):
                    scv = gidx * 4 + k
                    t.transpose(ps_vt[:, k * 128:(k + 1) * 128],
                                vt[:, scv * 128:(scv + 1) * 128], identr[:])
                t.drain().then_inc(sm["vtr"], 1)
            t.wait_ge(sm["vtre"], 4)
            t.wait_ge(sm["qkve"], 8)
            for qb in range(NQB):
                base = qb * 32
                qsl = slice(qb * 512, (qb + 1) * 512)
                if qb == 3:
                    t.wait_ge(sm["wabs"], n_wch + 1)
                    t.matmul(pv32[0:1, 0:4], onesf[:], wacc[:],
                             start=True, stop=True)
                    t.drain().then_inc(sm["wsum"], 1)
                if qb == 4:
                    t.wait_ge(sm["wscal"], 1)
                    t.matmul(pv32[:, 4:12], ones1f[:], srow[:],
                             start=True, stop=True)
                    t.drain().then_inc(sm["wbc"], 1)
                if qb >= 1:
                    sc3 = qb * 4 - 1
                    t.wait_ge(sm["xqd"], sc3 + 1)
                    t.wait_ge(sm["xtre"], sc3)
                    for ic in range(NHC):
                        t.transpose(ps_tr[:, ic * 128:(ic + 1) * 128],
                                    xqdq[:, sc3 % 2, ic * 128:(ic + 1) * 128],
                                    identb[:])
                    t.drain().then_inc(sm["xtr"], 1)
                    t.wait_ge(sm["hev"], qb)
                    t.wait_ge(sm["rdone"], qb)
                    t.wait_ge(sm["pexp"], base)
                for kc in range(32):
                    if kc >= 2:
                        t.wait_ge(sm["pexp"], base + kc - 1)
                    t.matmul(ps_s[:, kc % 2, :], kt[:, kc * 128:(kc + 1) * 128],
                             qt[:, qsl], start=True, stop=True
                             ).then_inc(sm["ss"], 1)
                    t.wait_ge(sm["pexp"], base + kc + 1)
                    t.matmul(ps_h[:], vtok[:, kc, :], pbuf[:, kc % 2, :],
                             start=(kc == 0), stop=(kc == 31))
                    t.matmul(ps_r[:], ones_r[:], pbuf[:, kc % 2, :],
                             start=(kc == 0), stop=(kc == 31)
                             ).then_inc(sm["avrs"], 1)
                t.wait_ge(sm["rdone"], qb + 1)
                if qb >= 1:
                    t.wait_ge(sm["z1"], 8 * qb)
                for j in range(4):
                    t.transpose(ps_y1[:, 0, j:j + 1],
                                rinv[0:1, j * 128:(j + 1) * 128],
                                identf[0:1, 0:1])
                t.drain().then_inc(sm["rcolT"], 1)
                t.wait_ge(sm["hev"], qb + 1)
                t.wait_ge(sm["rcole"], qb + 1)
                def _xtr(t, sc):
                    t.wait_ge(sm["xqd"], sc + 1)
                    if sc >= 1:
                        t.wait_ge(sm["xtre"], sc)
                    for ic in range(NHC):
                        t.transpose(ps_tr[:, ic * 128:(ic + 1) * 128],
                                    xqdq[:, sc % 2, ic * 128:(ic + 1) * 128],
                                    identb[:])
                    t.drain().then_inc(sm["xtr"], 1)
                for j in range(4):
                    sc = qb * 4 + j
                    if sc >= 1:
                        t.wait_ge(sm["z1"], 2 * sc)
                    t.matmul(ps_y1[:, 0, :], htb[:, j * 128:(j + 1) * 128],
                             wob[:, 0:512], start=True, stop=True)
                    t.matmul(ps_y1[:, 1, :], htb[:, j * 128:(j + 1) * 128],
                             wob[:, 512:1024], start=True, stop=True
                             ).then_inc(sm["y1"], 1)
                    if j >= 1:
                        _xtr(t, qb * 4 + j - 1)
                if qb == NQB - 1:
                    _xtr(t, qb * 4 + 3)


def _phase_C1(nc, sm, d, pers):
    identb = pers["identb"]; yscale = pers["yscale"]
    wsv = pers["wsv"]; stats = pers["stats"]; aqs = pers["aqs"]

    es = contextlib.ExitStack()
    def SB(name, shape, dt):
        return es.enter_context(nc.sbuf_tensor(name, shape, dt))
    def PS(name, shape, dt):
        return es.enter_context(nc.psum_tensor(name, shape, dt))
    with es:
        xqr = SB("xqr", [P, 2, NHC, 512], BF16)
        wgu = SB("wgu", [P, 2, 2, NHC, 512], BF16)
        hp = SB("hp", [P, SCQ, FF], F32)
        sgt = SB("sgt", [P, 2, 512], F32)
        gtr = SB("gtr", [P, 2, 512], F32)
        scv = SB("scv", [P, 8], F32)
        maxh = SB("maxh", [P, SCQ], F32)
        hq = SB("hq", [P, 2, FF], BF16)
        uq2 = SB("uq2", [P, 1024], F32)
        tiq2 = SB("tiq2", [P, 1024], I32)
        c1q2 = SB("c1q2", [P, 1024], F32)
        hqts = SB("hqts", [P, 2, NOC, DH], BF16)
        ps_g = PS("ps_g", [P, 2, 512], F32)
        ps_u = PS("ps_u", [P, 2, 512], F32)
        ps_t2 = PS("ps_t2", [P, 1024], BF16)
        blk = es.enter_context(nc.Block())
        @blk.sync
        def _(sy):
            sy.wait_ge(sm["wbc"], 1)
            sy.wait_ge(sm["xtre"], NSC)
            sy.wait_ge(sm["ldo0"], 16 * (NSC // 2))
            sy.wait_ge(sm["ldo1"], 16 * (NSC // 2))
            sy.wait_ge(sm["x1sp0"], 16 * (NSC // 2))
            sy.wait_ge(sm["x1sp1"], 16 * (NSC // 2))
            sy.wait_ge(sm["xqsp0"], 16 * NHC * (NQB // 2))
            sy.wait_ge(sm["xqsp1"], 16 * NHC * (NQB // 2))
            sy.wait_ge(sm["ldx"], 16 * NQB)
            for sb in range(NSB):
                if sb >= 2:
                    sy.wait_ge(sm["gu"], (sb - 1) * NOB * SCQ)
                for ic in range(NHC):
                    sy.dma_start(out=xqr[:, sb % 2, ic, :],
                                 in_=d["xqTs"][ic * P:(ic + 1) * P,
                                               sb * 512:(sb + 1) * 512]
                                 ).then_inc(sm["ldxq0" if sb % 2 == 0
                                            else "ldxq1"], 16)
                for ob in range(NOB):
                    idx = sb * NOB + ob
                    if sb == 0:
                        grp = ob // 2
                        need = 16 * (grp + 1)
                        sy.wait_ge(sm["wqsp0"], 16 * (need // 2))
                        sy.wait_ge(sm["wqsp1"], 16 * (need // 2))
                    if idx >= 2:
                        sy.wait_ge(sm["gu"], (idx - 1) * SCQ)
                    for w_i, wsrc in enumerate((d["wgqs"], d["wuqs"])):
                        sy.dma_start(
                            out=wgu[:, idx % 2, w_i, :, :],
                            in_=wsrc.rearrange("(a p) f -> p a f", p=P)[
                                :, :, ob * 512:(ob + 1) * 512]
                        ).then_inc(sm["ldgu0" if idx % 2 == 0
                                   else "ldgu1"], 16)

        @blk.gpsimd
        def _(gp):
            gp = _AD(gp)
            for gsc in range(NSC):
                gp.wait_ge(sm["htre"], (gsc + 1) * 4)
                gp.dma_start(
                    out=d["hqTs"].rearrange("(a p) s -> p a s", p=P)[
                        :, :, gsc * 128:(gsc + 1) * 128],
                    in_=hqts[:, gsc % 2, :, :]).then_inc(sm["hqspill"], 16)
                gp.wait_ge(sm["hqspill"], 16 * (gsc + 1))

        @blk.vector
        def _(v):
            v = _AD(v)
            for sb in range(NSB):
                for sc in range(SCQ):
                    gsc = sb * SCQ + sc
                    v.memset(maxh[:, sc:sc + 1], 0.0)
                    v.tensor_scalar(scv[:, 2 * sc:2 * sc + 1],
                                    aqs[:, gsc:gsc + 1], wsv[:, 0:1], None,
                                    ALU.mult)
                    v.tensor_scalar(scv[:, 2 * sc + 1:2 * sc + 2],
                                    aqs[:, gsc:gsc + 1], wsv[:, 1:2], None,
                                    ALU.mult)
                for ob in range(NOB):
                    for sc in range(SCQ):
                        k = (sb * NOB + ob) * SCQ + sc
                        v.wait_ge(sm["gu"], k + 1)
                        if k >= 2:
                            v.wait_ge(sm["silu"], k - 1)
                        v.tensor_scalar(gtr[:, sc % 2, :], ps_g[:, sc % 2, :],
                                        scv[:, 2 * sc:2 * sc + 1], None,
                                        ALU.mult).then_inc(sm["gpre"], 1)
                        v.wait_ge(sm["silu"], k + 1)
                        v.tensor_tensor(hp[:, sc, ob * 512:(ob + 1) * 512],
                                        sgt[:, sc % 2, :], ps_u[:, sc % 2, :],
                                        ALU.mult)
                        v.reduce_max(stats[:, 15:16],
                                     hp[:, sc, ob * 512:(ob + 1) * 512], AX.X,
                                     apply_absolute_value=True)
                        v.tensor_tensor(maxh[:, sc:sc + 1], maxh[:, sc:sc + 1],
                                        stats[:, 15:16], ALU.max
                                        ).then_inc(sm["mid"], 1)
                for sc in range(SCQ):
                    gsc = sb * SCQ + sc
                    v.tensor_scalar(stats[:, 0:1], maxh[:, sc:sc + 1],
                                    scv[:, 2 * sc + 1:2 * sc + 2], 1e-5,
                                    ALU.mult, ALU.max)
                    v.reciprocal(stats[:, 1:2], stats[:, 0:1])
                    v.scalar_tensor_tensor(stats[:, 2:3], stats[:, 1:2], 127.0,
                                           scv[:, 2 * sc + 1:2 * sc + 2],
                                           ALU.mult, ALU.mult)
                    v.tensor_scalar(stats[:, 3:4], stats[:, 0:1],
                                    INV127_LO, None, ALU.mult)
                    v.scalar_tensor_tensor(stats[:, 3:4], stats[:, 0:1],
                                           INV127_HI, stats[:, 3:4],
                                           ALU.mult, ALU.add)
                    v.tensor_scalar(yscale[:, gsc:gsc + 1], stats[:, 3:4],
                                    wsv[:, 2:3], None, ALU.mult)
                    if gsc >= 2:
                        v.wait_ge(sm["htr"], (gsc - 1) * 4)
                    for hf in range(4):
                        hsl = slice(hf * 1024, (hf + 1) * 1024)
                        v.tensor_scalar(uq2[:], hp[:, sc, hsl], stats[:, 2:3],
                                        128.0, ALU.mult, ALU.add)
                        v.tensor_copy(tiq2[:], uq2[:])
                        v.scalar_tensor_tensor(c1q2[:], uq2[:], -0.5, tiq2[:],
                                               ALU.add, ALU.is_ge)
                        v.scalar_tensor_tensor(hq[:, gsc % 2, hsl], tiq2[:],
                                               -128.0, c1q2[:], ALU.add,
                                               ALU.add)
                    v.drain().then_inc(sm["rnd"], 1)

        @blk.scalar
        def _(s):
            s = _AD(s)
            for sb in range(NSB):
                for ob in range(NOB):
                    for sc in range(SCQ):
                        k = (sb * NOB + ob) * SCQ + sc
                        s.wait_ge(sm["gpre"], k + 1)
                        if k >= 2:
                            s.wait_ge(sm["mid"], k - 1)
                        s.activation(sgt[:, sc % 2, :], gtr[:, sc % 2, :],
                                     ACTF.Silu, bias=0.0, scale=1.0
                                     ).then_inc(sm["silu"], 1)
                for sc in range(SCQ):
                    gsc = sb * SCQ + sc
                    for gi in range(4):
                        s.wait_ge(sm["htr"], gsc * 4 + gi + 1)
                        if gsc >= 2 and gi == 0:
                            s.wait_ge(sm["hqspill"], 16 * (gsc - 1))
                        s.copy(hqts[:, gsc % 2, gi * 8:(gi + 1) * 8, :],
                               ps_t2[:].rearrange("p (a b) -> p a b", a=8)
                               ).then_inc(sm["htre"], 1)

        @blk.tensor
        def _(t):
            for sb in range(NSB):
                t.wait_ge(sm["ldxq0" if sb % 2 == 0 else "ldxq1"],
                          16 * NHC * (sb // 2 + 1))
                for ob in range(NOB):
                    idx = sb * NOB + ob
                    t.wait_ge(sm["ldgu0" if idx % 2 == 0 else "ldgu1"],
                              16 * 2 * (idx // 2 + 1))
                    for sc in range(SCQ):
                        k = idx * SCQ + sc
                        if k >= 2:
                            t.wait_ge(sm["gpre"], k - 1)
                            t.wait_ge(sm["mid"], k - 1)
                        for ic in range(NHC):
                            t.matmul(ps_g[:, sc % 2, :],
                                     xqr[:, sb % 2, ic,
                                         sc * 128:(sc + 1) * 128],
                                     wgu[:, idx % 2, 0, ic, :],
                                     start=(ic == 0), stop=(ic == NHC - 1))
                        for ic in range(NHC):
                            t.matmul(ps_u[:, sc % 2, :],
                                     xqr[:, sb % 2, ic,
                                         sc * 128:(sc + 1) * 128],
                                     wgu[:, idx % 2, 1, ic, :],
                                     start=(ic == 0), stop=(ic == NHC - 1))
                        t.drain().then_inc(sm["gu"], 1)
                for sc in range(SCQ):
                    gsc = sb * SCQ + sc
                    t.wait_ge(sm["rnd"], gsc + 1)
                    for gi in range(4):
                        if gsc * 4 + gi >= 1:
                            t.wait_ge(sm["htre"], gsc * 4 + gi)
                        for k in range(8):
                            oc = gi * 8 + k
                            t.transpose(ps_t2[:, k * 128:(k + 1) * 128],
                                        hq[:, gsc % 2,
                                           oc * 128:(oc + 1) * 128],
                                        identb[:])
                        t.drain().then_inc(sm["htr"], 1)


def _phase_C2(nc, sm, d, pers):
    yscale = pers["yscale"]; stats = pers["stats"]

    es = contextlib.ExitStack()
    def SB(name, shape, dt):
        return es.enter_context(nc.sbuf_tensor(name, shape, dt))
    def PS(name, shape, dt):
        return es.enter_context(nc.psum_tensor(name, shape, dt))
    with es:
        wdq = SB("wdq", [P, NOC, D], BF16)
        hqr = SB("hqr", [P, 4, NOC, 2 * DH], BF16)
        x1r = SB("x1r", [P, 2, D], F32)
        z2t = SB("z2t", [P, D], F32)
        sq2 = SB("sq2", [P, D], F32)
        outt = SB("outt", [P, 2, D], F32)
        ps_y2 = PS("ps_y2", [P, 3, 2, 512], F32)
        blk = es.enter_context(nc.Block())
        @blk.sync
        def _(sy):
            sy.wait_ge(sm["wqsp0"], 16 * 48)
            sy.wait_ge(sm["wqsp1"], 16 * 48)
            sy.wait_ge(sm["gu"], NSB * NOB * SCQ)
            sy.wait_ge(sm["rnd"], NSC)
            sy.wait_ge(sm["ldxq0"], 16 * NHC * (NSB // 2))
            sy.wait_ge(sm["ldxq1"], 16 * NHC * (NSB // 2))
            sy.wait_ge(sm["ldgu0"], 16 * 2 * (NSB * NOB // 2))
            sy.wait_ge(sm["ldgu1"], 16 * 2 * (NSB * NOB // 2))
            def ld_pair(pr):
                sc = 2 * pr
                sy.wait_ge(sm["hqspill"], 16 * (sc + 2))
                if sc >= 8:
                    sy.wait_ge(sm["dwn"], sc - 6)
                for half in range(2):
                    sy.dma_start(
                        out=hqr[:, pr % 4, half * 16:(half + 1) * 16, :],
                        in_=d["hqTs"].rearrange("(a p) s -> p a s", p=P)[
                            :, half * 16:(half + 1) * 16,
                            sc * 128:(sc + 2) * 128]
                    ).then_inc(sm["ldhq%d" % (pr % 4)], 16)
            ld_pair(0)
            ld_pair(1)
            for oc in range(NOC):
                sy.dma_start(out=wdq[:, oc, :],
                             in_=d["wdqs"][oc * P:(oc + 1) * P, :]
                             ).then_inc(sm["ldwd"], 16)
                if oc % 8 == 7:
                    sy.wait_ge(sm["ldwd"], 16 * (oc + 1))
            for sc in range(4, NSC, 2):
                ld_pair(sc // 2)
            sy.wait_ge(sm["outd0"], 16 * (NSC // 2))
            sy.wait_ge(sm["outd1"], 16 * (NSC // 2))

        @blk.gpsimd
        def _(gp):
            gp = _AD(gp)
            def out_dma(sc):
                gp.wait_ge(sm["outv"], sc + 1)
                gp.dma_start(out=d["out"][sc * P:(sc + 1) * P, :],
                             in_=outt[:, sc % 2, :]
                             ).then_inc(sm["outd0" if sc % 2 == 0
                                        else "outd1"], 16)
            for sc in range(NSC):
                if sc >= 2:
                    gp.wait_ge(sm["z2st"], 2 * (sc - 1))
                gp.dma_start(out=x1r[:, sc % 2, :],
                             in_=d["x1s"][sc * P:(sc + 1) * P, :]
                             ).then_inc(sm["ldx10" if sc % 2 == 0
                                        else "ldx11"], 16)
                if sc >= 2:
                    out_dma(sc - 2)
            out_dma(NSC - 2)
            out_dma(NSC - 1)

        @blk.vector
        def _(v):
            v = _AD(v)
            for sc in range(NSC):
                v.wait_ge(sm["dwn"], sc + 1)
                v.wait_ge(sm["ldx10" if sc % 2 == 0 else "ldx11"],
                          16 * (sc // 2 + 1))
                v.scalar_tensor_tensor(z2t[:, 0:512], ps_y2[:, sc % 3, 0, :],
                                       yscale[:, sc:sc + 1],
                                       x1r[:, sc % 2, 0:512],
                                       ALU.mult, ALU.add)
                v.scalar_tensor_tensor(z2t[:, 512:1024], ps_y2[:, sc % 3, 1, :],
                                       yscale[:, sc:sc + 1],
                                       x1r[:, sc % 2, 512:1024],
                                       ALU.mult, ALU.add).then_inc(sm["z2st"], 1)
                v.reduce_sum(stats[:, 0:1], z2t[:], AX.X)
                v.tensor_tensor(sq2[:], z2t[:], z2t[:], ALU.mult)
                v.reduce_sum(stats[:, 1:2], sq2[:], AX.X)
                v.tensor_scalar(stats[:, 2:3], stats[:, 0:1], INVD, None,
                                ALU.mult)
                v.tensor_tensor(stats[:, 3:4], stats[:, 2:3], stats[:, 2:3],
                                ALU.mult)
                v.scalar_tensor_tensor(stats[:, 4:5], stats[:, 1:2], INVD,
                                       stats[:, 3:4], ALU.mult, ALU.subtract)
                v.tensor_scalar(stats[:, 4:5], stats[:, 4:5], 1.0, EPS,
                                ALU.mult, ALU.add).then_inc(sm["z2st"], 1)
                v.wait_ge(sm["sqrt2"], sc + 1)
                v.reciprocal(stats[:, 6:7], stats[:, 5:6])
                for _i in range(2):
                    v.tensor_tensor(stats[:, 7:8], stats[:, 6:7],
                                    stats[:, 6:7], ALU.mult)
                    v.tensor_tensor(stats[:, 7:8], stats[:, 7:8],
                                    stats[:, 4:5], ALU.mult)
                    v.tensor_scalar(stats[:, 7:8], stats[:, 7:8], -0.5, 1.5,
                                    ALU.mult, ALU.add)
                    v.tensor_tensor(stats[:, 6:7], stats[:, 6:7],
                                    stats[:, 7:8], ALU.mult)
                if sc >= 2:
                    v.wait_ge(sm["outd0" if sc % 2 == 0 else "outd1"],
                              16 * (sc // 2))
                v.tensor_scalar(outt[:, sc % 2, :], z2t[:], stats[:, 2:3],
                                stats[:, 6:7], ALU.subtract, ALU.mult
                                ).then_inc(sm["outv"], 1)

        @blk.scalar
        def _(s):
            s = _AD(s)
            for sc in range(NSC):
                s.wait_ge(sm["z2st"], 2 * (sc + 1))
                s.activation(stats[:, 5:6], stats[:, 4:5], ACTF.Sqrt,
                             bias=0.0, scale=1.0).then_inc(sm["sqrt2"], 1)

        @blk.tensor
        def _(t):
            for sc in range(NSC):
                pr = sc // 2
                t.wait_ge(sm["ldhq%d" % (pr % 4)], 16 * 2 * (pr // 4 + 1))
                if sc >= 3:
                    t.wait_ge(sm["z2st"], 2 * (sc - 2))
                for hf in range(2):
                    for oc in range(NOC):
                        if sc == 0 and hf == 0 and oc % 8 == 0:
                            t.wait_ge(sm["ldwd"], 16 * (oc + 8))
                        t.matmul(ps_y2[:, sc % 3, hf, :],
                                 hqr[:, pr % 4, oc,
                                     (sc % 2) * DH:(sc % 2 + 1) * DH],
                                 wdq[:, oc, hf * 512:(hf + 1) * 512],
                                 start=(oc == 0), stop=(oc == NOC - 1))
                t.drain().then_inc(sm["dwn"], 1)


def kernel(L0_attn_x, L0_attn_head_dim_mask, L0_attn_w_q, L0_attn_w_k,
           L0_attn_w_v, L0_attn_w_o, L0_ffn_w_gate, L0_ffn_w_up,
           L0_ffn_w_down):
    x = np.ascontiguousarray(np.asarray(L0_attn_x, dtype=np.float32))
    B = x.shape[0]
    xT = np.ascontiguousarray(x.transpose(0, 2, 1))
    mask = np.ascontiguousarray(
        np.asarray(L0_attn_head_dim_mask, dtype=np.int32).reshape(DH, 1))
    w_q = np.ascontiguousarray(np.asarray(L0_attn_w_q, dtype=np.float32))
    w_k = np.ascontiguousarray(np.asarray(L0_attn_w_k, dtype=np.float32))
    w_v = np.ascontiguousarray(np.asarray(L0_attn_w_v, dtype=np.float32))
    w_o = np.ascontiguousarray(np.asarray(L0_attn_w_o, dtype=np.float32))
    wgT = np.ascontiguousarray(np.asarray(L0_ffn_w_gate, dtype=np.float32).T)
    wuT = np.ascontiguousarray(np.asarray(L0_ffn_w_up, dtype=np.float32).T)
    wdT = np.ascontiguousarray(np.asarray(L0_ffn_w_down, dtype=np.float32).T)

    nc = build_program()
    in_maps = [{
        "xT": xT[b], "x": x[b], "w_q": w_q, "w_k": w_k, "w_v": w_v,
        "w_o": w_o, "mask": mask, "wgT": wgT, "wuT": wuT, "wdT": wdT,
    } for b in range(B)]
    res = run_bass_kernel_spmd(nc, in_maps, core_ids=list(range(B)))
    out = np.stack([np.asarray(r["out"]) for r in res.results], axis=0)
    return out if out.dtype == np.float64 else out.astype(np.float32)


# version 15 (final)
# speedup vs baseline: 1.0105x; 1.0105x over previous
import contextlib

import numpy as np
import concourse.bass as bass
import concourse.mybir as mybir
from concourse.bass_utils import run_bass_kernel_spmd

F32 = mybir.dt.float32
F32R = mybir.dt.float32r
BF16 = mybir.dt.bfloat16
I32 = mybir.dt.int32
ALU = mybir.AluOpType
ACTF = mybir.ActivationFunctionType
AX = mybir.AxisListType

P = 128
S = 4096
D = 1024
DH = 128
FF = 4096
NSC = S // P
NQB = 8
NHC = D // P
NOC = FF // P
NOB = 8
SBLK = 512
NSB = S // SBLK
SCQ = 4
ATT_SCALE = float(1.0 / np.sqrt(128.0))
EPS = 1e-5
INV127 = float(1.0 / 127.0)
INV127_HI = float(np.float32(1.0 / 127.0))
INV127_LO = float(np.float32(1.0 / 127.0 - np.float64(np.float32(1.0 / 127.0))))
INVD = float(1.0 / D)
WMEAN_N = float(1.0 / (D * FF))
C0GU = float(np.float32(np.sqrt(2.0 / (np.pi * 1024.0))))
C0D = float(np.float32(np.sqrt(2.0 / (np.pi * 4096.0))))



class _AD:

    _PASS = {"wait_ge", "dma_start", "drain", "wait_op"}

    def __init__(self, e):
        self._e = e

    def __getattr__(self, n):
        f = getattr(self._e, n)
        if n in self._PASS or not callable(f):
            return f
        def g(*a, **k):
            f(*a, **k)
            return self._e.drain()
        return g


class Sems:
    def __init__(self, nc, names):
        self._h = {n: nc.alloc_semaphore(name=n) for n in names}

    def __getitem__(self, n):
        return self._h[n]


SEM_NAMES = [
    "idn", "ldw", "ldx", "ldwb", "ldo0", "ldo1", "wst0", "wst1",
    "x1sp0", "x1sp1", "xqsp0", "xqsp1", "wqsp0", "wqsp1",
    "ldxq0", "ldxq1", "ldgu0", "ldgu1", "ldhq0", "ldhq1", "ldhq2", "ldhq3",
    "ldx10", "ldx11",
    "outd0", "outd1", "qkv", "qkve", "vtr", "vtre",
    "ss", "pexp", "avrs", "hev", "rdone", "rcolT", "rcole", "y1",
    "ldo", "z1", "sqrt1", "x1q", "xqd", "xtr", "xtre", "xqspill", "x1spill",
    "wst", "wabs", "wsum", "wscal", "wbc", "wsc", "wqz", "wqspill",
    "ldxq", "ldgu", "gu", "gpre", "silu", "mid", "rnd", "htr", "htre", "hqspill",
    "ldhq", "ldwd", "ldx1", "dwn", "sqrt2", "z2st", "outst", "outv",
]


def build_program():
    nc = bass.Bass("TRN2", target_bir_lowering=False, debug=False)

    d = {}
    d["xT"] = nc.dram_tensor("xT", [D, S], F32R, kind="ExternalInput").ap()
    d["x"] = nc.dram_tensor("x", [S, D], F32, kind="ExternalInput").ap()
    d["w_q"] = nc.dram_tensor("w_q", [D, DH], F32, kind="ExternalInput").ap()
    d["w_k"] = nc.dram_tensor("w_k", [D, DH], F32, kind="ExternalInput").ap()
    d["w_v"] = nc.dram_tensor("w_v", [D, DH], F32, kind="ExternalInput").ap()
    d["w_o"] = nc.dram_tensor("w_o", [DH, D], F32, kind="ExternalInput").ap()
    d["mask"] = nc.dram_tensor("mask", [DH, 1], I32, kind="ExternalInput").ap()
    d["wgT"] = nc.dram_tensor("wgT", [D, FF], F32, kind="ExternalInput").ap()
    d["wuT"] = nc.dram_tensor("wuT", [D, FF], F32, kind="ExternalInput").ap()
    d["wdT"] = nc.dram_tensor("wdT", [FF, D], F32, kind="ExternalInput").ap()
    d["out"] = nc.dram_tensor("out", [S, D], F32, kind="ExternalOutput").ap()
    d["x1s"] = nc.dram_tensor("x1s", [S, D], F32, kind="Internal").ap()
    d["xqTs"] = nc.dram_tensor("xqTs", [D, S], BF16, kind="Internal").ap()
    d["wgqs"] = nc.dram_tensor("wgqs", [D, FF], BF16, kind="Internal").ap()
    d["wuqs"] = nc.dram_tensor("wuqs", [D, FF], BF16, kind="Internal").ap()
    d["wdqs"] = nc.dram_tensor("wdqs", [FF, D], BF16, kind="Internal").ap()
    d["hqTs"] = nc.dram_tensor("hqTs", [FF, S], BF16, kind="Internal").ap()

    sm = Sems(nc, SEM_NAMES)

    with (
        nc.sbuf_tensor("ones_b", [P, 1], BF16) as ones_b,
        nc.sbuf_tensor("onesf", [P, 1], F32) as onesf,
        nc.sbuf_tensor("ones1f", [1, P], F32) as ones1f,
        nc.sbuf_tensor("identb", [P, P], BF16) as identb,
        nc.sbuf_tensor("identf", [P, P], F32) as identf,
        nc.sbuf_tensor("wsv", [P, 8], F32) as wsv,
        nc.sbuf_tensor("yscale", [P, NSC], F32) as yscale,
        nc.sbuf_tensor("aqs", [P, NSC], F32) as aqs,
        nc.sbuf_tensor("srow", [1, 8], F32) as srow,
        nc.sbuf_tensor("stats", [P, 16], F32) as stats,
        nc.sbuf_tensor("wstf", [P, 2, 1024], F32) as wstf,
        nc.sbuf_tensor("wacc", [P, 4], F32) as wacc,
        nc.sbuf_tensor("wpar", [P, 3, 32], F32) as wpar,
        nc.sbuf_tensor("crow", [1, 4], F32) as crow,
        nc.sbuf_tensor("wqzb", [P, 2, 1024], BF16) as wqzb,
        nc.sbuf_tensor("wu_", [P, 1024], F32) as wu_,
        nc.sbuf_tensor("wc2", [P, 1024], F32) as wc2,
        nc.sbuf_tensor("wc1", [P, 1024], F32) as wc1,
    ):
        pers = dict(ones_b=ones_b, onesf=onesf, ones1f=ones1f, identb=identb,
                    identf=identf, wsv=wsv, yscale=yscale, aqs=aqs, srow=srow,
                    stats=stats, wstf=wstf, wacc=wacc, wpar=wpar, crow=crow,
                    wqzb=wqzb, wu_=wu_, wc2=wc2, wc1=wc1)
        _phase_AB(nc, sm, d, pers)
        _phase_C1(nc, sm, d, pers)
        _phase_C2(nc, sm, d, pers)
    return nc


def _phase_AB(nc, sm, d, pers):
    ones_b = pers["ones_b"]; onesf = pers["onesf"]; ones1f = pers["ones1f"]
    identb = pers["identb"]; identf = pers["identf"]
    wsv = pers["wsv"]; srow = pers["srow"]; stats = pers["stats"]
    aqs = pers["aqs"]
    wstf = pers["wstf"]; wacc = pers["wacc"]; wpar = pers["wpar"]
    crow = pers["crow"]; wqzb = pers["wqzb"]; wu_ = pers["wu_"]
    wc2 = pers["wc2"]; wc1 = pers["wc1"]

    wplan = [(d["wgT"], d["wgqs"], NHC, 4, 1024, 0),
             (d["wuT"], d["wuqs"], NHC, 4, 1024, 1),
             (d["wdT"], d["wdqs"], NOC, 1, 1024, 2)]
    n_wch = sum(r * c for _, _, r, c, _, _ in wplan)
    wsched = []
    for c in range(4):
        for src, wi in ((d["wgT"], 0), (d["wuT"], 1)):
            for r in range(NHC):
                wsched.append((src, r, c, 1024, wi))
    for r in range(NOC):
        wsched.append((d["wdT"], r, 0, 1024, 2))
    wdsts = [d["wgqs"], d["wuqs"], d["wdqs"]]
    _cnt = [0, 0, 0]
    wseq = []
    for src, r, c, n, wi in wsched:
        wseq.append(_cnt[wi])
        _cnt[wi] += 1

    es = contextlib.ExitStack()
    def SB(name, shape, dt):
        return es.enter_context(nc.sbuf_tensor(name, shape, dt))
    def PS(name, shape, dt):
        return es.enter_context(nc.psum_tensor(name, shape, dt))
    with es:
        xts = SB("xts", [P, 2, NHC, 512], F32R)
        vt = SB("vt", [P, S], F32R)
        qt = SB("qt", [P, S], F32R)
        kt = SB("kt", [P, S], F32R)
        vtok = SB("vtok", [P, NSC, DH], F32R)
        wqkvb = SB("wqkvb", [P, NHC, 3 * DH], F32R)
        wof = SB("wof", [P, D], F32)
        wob = SB("wob", [P, D], F32R)
        maskf = SB("maskf", [P, 1], F32)
        pbuf = SB("pbuf", [P, 2, 512], F32R)
        htb = SB("htb", [P, 512], F32R)
        rsb = SB("rsb", [1, 512], F32)
        rinv = SB("rinv", [1, 512], F32)
        rcol = SB("rcol", [P, 4], F32)
        xo = SB("xo", [P, 2, D], F32)
        z1t = SB("z1t", [P, D], F32)
        tmp1 = SB("tmp1", [P, D], F32)
        x1t = SB("x1t", [P, 2, D], F32)
        tiq = SB("tiq", [P, D], I32)
        c1q = SB("c1q", [P, D], F32)
        xqdq = SB("xqdq", [P, 2, D], BF16)
        xqs = SB("xqs", [P, NHC, 512], BF16)
        ps_s = PS("ps_s", [P, 2, 512], F32)
        ps_h = PS("ps_h", [P, 512], F32)
        ps_r = PS("ps_r", [1, 512], F32)
        ps_y1 = PS("ps_y1", [P, 2, 512], F32)
        ps_vt = PS("ps_vt", [P, 512], F32R)
        ps_tr = PS("ps_tr", [P, 1024], BF16)
        identr = SB("identr", [P, P], F32R)
        ones_r = SB("ones_r", [P, 1], F32R)
        whm = SB("whm", [P, 8], F32)
        pv32 = ps_vt[:].bitcast(F32)
        blk = es.enter_context(nc.Block())
        @blk.sync
        def _(sy):
            sy.dma_start(out=wof[:], in_=d["w_o"]).then_inc(sm["ldw"], 16)
            for ch, (src, r, c, n, wi) in enumerate(wsched):
                if ch >= 2:
                    sy.wait_ge(sm["wabs"], ch - 1)
                sy.dma_start(
                    out=wstf[:, ch % 2, 0:n],
                    in_=src[r * P:(r + 1) * P, c * 1024:c * 1024 + n]
                ).then_inc(sm["wst0" if ch % 2 == 0 else "wst1"], 16)
            sy.wait_ge(sm["wsc"], 1)
            for ch, (src, r, c, n, wi) in enumerate(wsched):
                if ch >= 2:
                    sy.wait_ge(sm["wqz"], ch - 1)
                sy.dma_start(
                    out=wstf[:, ch % 2, 0:n],
                    in_=src[r * P:(r + 1) * P, c * 1024:c * 1024 + n]
                ).then_inc(sm["wst0" if ch % 2 == 0 else "wst1"], 16)

        @blk.gpsimd
        def _(gp):
            gp = _AD(gp)
            gp.wait_ge(sm["idn"], 1)
            gp.affine_select(out=identb[:], in_=identb[:],
                             compare_op=ALU.not_equal, fill=1.0, base=0,
                             pattern=[[-1, P]], channel_multiplier=1)
            gp.affine_select(out=identf[:], in_=identf[:],
                             compare_op=ALU.not_equal, fill=1.0, base=0,
                             pattern=[[-1, P]], channel_multiplier=1)
            gp.drain().then_inc(sm["idn"], 1)
            for i, w in enumerate((d["w_q"], d["w_k"], d["w_v"])):
                for h in range(NHC):
                    gp.dma_start(out=wqkvb[:, h, i * DH:(i + 1) * DH],
                                 in_=w[h * P:(h + 1) * P, :]
                                 ).then_inc(sm["ldwb"], 16)
            gp.dma_start(out=maskf[:], in_=d["mask"]).then_inc(sm["ldwb"], 16)
            for sb in range(NQB):
                if sb >= 2:
                    gp.wait_ge(sm["qkv"], 3 * (sb - 1))
                gp.dma_start(out=xts[:, sb % 2, :, :],
                             in_=d["xT"].rearrange("(a p) s -> p a s", p=P)[
                                 :, :, sb * 512:(sb + 1) * 512]
                             ).then_inc(sm["ldx"], 16)
                gp.wait_ge(sm["ldx"], 16 * (sb + 1))
            def _xo(sc):
                if sc >= NSC:
                    return
                if sc >= 2:
                    gp.wait_ge(sm["z1"], 2 * (sc - 1))
                gp.dma_start(out=xo[:, sc % 2, :],
                             in_=d["x"][sc * P:(sc + 1) * P, :]
                             ).then_inc(sm["ldo0" if sc % 2 == 0 else "ldo1"], 16)
            _xo(0)
            _xo(1)
            _wsp = [0]
            def wsp_chunks(k):
                while k > 0 and _wsp[0] < len(wsched):
                    ch = _wsp[0]
                    src, r, c, n, wi = wsched[ch]
                    dst = wdsts[wi]
                    gp.wait_ge(sm["wqz"], ch + 1)
                    gp.dma_start(
                        out=dst[r * P:(r + 1) * P, c * 1024:c * 1024 + n],
                        in_=wqzb[:, ch % 2, 0:n]
                    ).then_inc(sm["wqsp0" if ch % 2 == 0 else "wqsp1"], 16)
                    _wsp[0] += 1
                    k -= 1
            for sc in range(NSC):
                gp.wait_ge(sm["x1q"], sc + 1)
                gp.dma_start(out=d["x1s"][sc * P:(sc + 1) * P, :],
                             in_=x1t[:, sc % 2, :]
                             ).then_inc(sm["x1sp0" if sc % 2 == 0 else "x1sp1"], 16)
                if sc % 4 == 3:
                    qb = sc // 4
                    gp.wait_ge(sm["xtre"], (qb + 1) * 4)
                    for ic in range(NHC):
                        gp.dma_start(
                            out=d["xqTs"][ic * P:(ic + 1) * P,
                                          qb * 512:(qb + 1) * 512],
                            in_=xqs[:, ic, :]
                        ).then_inc(sm["xqsp0" if qb % 2 == 0 else "xqsp1"], 16)
                    gp.wait_ge(sm["xqsp0" if qb % 2 == 0 else "xqsp1"],
                               16 * NHC * (qb // 2 + 1))
                _xo(sc + 2)
                if sc >= 16:
                    wsp_chunks(6)
            wsp_chunks(len(wsched))

        @blk.vector
        def _(v):
            v = _AD(v)
            v.memset(identb[:], 0.0)
            v.memset(identf[:], 0.0)
            v.memset(ones_b[:], 1.0)
            v.memset(onesf[:], 1.0)
            v.memset(ones1f[:], 1.0)
            v.memset(wacc[:], 0.0)
            v.drain().then_inc(sm["idn"], 1)
            v.wait_ge(sm["idn"], 2)
            v.tensor_copy(identr[:], identf[:])
            v.tensor_copy(ones_r[:], onesf[:])
            v.drain().then_inc(sm["idn"], 1)
            v.wait_ge(sm["ldw"], 16)
            v.wait_ge(sm["ldwb"], 16 * 25)
            v.tensor_scalar(wof[:], wof[:], maskf[:], None, ALU.mult)
            v.tensor_copy(wob[:], wof[:])
            v.memset(crow[0:1, 0:1], C0GU)
            v.memset(crow[0:1, 1:2], C0GU)
            v.memset(crow[0:1, 2:3], C0D)
            v.memset(crow[0:1, 3:4], 0.0)
            _w1 = [0]
            def w1_chunks(k):
                while k > 0 and _w1[0] < len(wsched):
                    ch = _w1[0]
                    src, r, c, n, wi = wsched[ch]
                    c0 = C0GU if wi < 2 else C0D
                    v.wait_ge(sm["wst0" if ch % 2 == 0 else "wst1"],
                              16 * (ch // 2 + 1))
                    v.reduce_sum(stats[:, 14:15], wstf[:, ch % 2, 0:n],
                                 AX.X, apply_absolute_value=True)
                    wch = wseq[ch]
                    v.tensor_scalar(wpar[:, wi, wch:wch + 1],
                                    stats[:, 14:15], 1.0, -float(n) * c0,
                                    ALU.mult, ALU.add
                                    ).then_inc(sm["wabs"], 1)
                    _w1[0] += 1
                    k -= 1
            _w2 = [0]
            def w2_chunks(k):
                while k > 0 and _w2[0] < len(wsched):
                    ch = _w2[0]
                    src, r, c, n, wi = wsched[ch]
                    v.wait_ge(sm["wst0" if ch % 2 == 0 else "wst1"],
                              16 * (len(wsched) // 2 + ch // 2 + 1))
                    wch = wstf[:, ch % 2, 0:n]
                    v.tensor_scalar(wc1[:, 0:n], wch, whm[:, wi:wi + 1],
                                    None, ALU.is_gt)
                    v.tensor_scalar(wc2[:, 0:n], wch, whm[:, 4 + wi:5 + wi],
                                    None, ALU.is_lt)
                    if ch >= 2:
                        v.wait_ge(sm["wqsp0" if ch % 2 == 0 else "wqsp1"],
                                  16 * (ch // 2))
                    v.tensor_tensor(wqzb[:, ch % 2, 0:n], wc1[:, 0:n],
                                    wc2[:, 0:n], ALU.subtract
                                    ).then_inc(sm["wqz"], 1)
                    _w2[0] += 1
                    k -= 1
            for qb in range(NQB):
                if qb <= 2:
                    w1_chunks(32)
                if qb == 2:
                    for wi in range(3):
                        v.reduce_sum(wacc[:, wi:wi + 1], wpar[:, wi, :], AX.X)
                    v.drain().then_inc(sm["wabs"], 1)
                if qb == 3:
                    v.wait_ge(sm["wsum"], 1)
                    v.tensor_copy(srow[0:1, 0:4], pv32[0:1, 0:4])
                    v.scalar_tensor_tensor(srow[0:1, 0:4], srow[0:1, 0:4],
                                           WMEAN_N, crow[0:1, 0:4],
                                           ALU.mult, ALU.add)
                    v.tensor_scalar(srow[0:1, 0:4], srow[0:1, 0:4], 1.0, 1e-5,
                                    ALU.mult, ALU.max)
                    v.reciprocal(srow[0:1, 4:8], srow[0:1, 0:4]
                                 ).then_inc(sm["wscal"], 1)
                if qb == 4:
                    v.wait_ge(sm["wbc"], 1)
                    v.tensor_copy(wsv[:], pv32[:, 4:12]).then_inc(sm["wsc"], 1)
                    v.tensor_scalar(whm[:, 0:4], wsv[:, 0:4], 0.5, None,
                                    ALU.mult)
                    v.tensor_scalar(whm[:, 4:8], wsv[:, 0:4], -0.5, None,
                                    ALU.mult)
                v.wait_ge(sm["avrs"], (qb + 1) * 32)
                v.tensor_copy(rsb[:], ps_r[:])
                v.reciprocal(rinv[:], rsb[:]).then_inc(sm["rdone"], 1)
                v.wait_ge(sm["rcolT"], qb + 1)
                v.tensor_copy(rcol[:], ps_y1[:, 0, 0:4]).then_inc(sm["rcole"], 1)
                for j in range(4):
                    sc = qb * 4 + j
                    v.wait_ge(sm["y1"], sc + 1)
                    v.wait_ge(sm["ldo0" if sc % 2 == 0 else "ldo1"],
                              16 * (sc // 2 + 1))
                    v.scalar_tensor_tensor(z1t[:, 0:512], ps_y1[:, 0, :],
                                           rcol[:, j:j + 1],
                                           xo[:, sc % 2, 0:512],
                                           ALU.mult, ALU.add)
                    v.scalar_tensor_tensor(z1t[:, 512:1024], ps_y1[:, 1, :],
                                           rcol[:, j:j + 1],
                                           xo[:, sc % 2, 512:1024],
                                           ALU.mult, ALU.add
                                           ).then_inc(sm["z1"], 1)
                    v.reduce_sum(stats[:, 0:1], z1t[:], AX.X)
                    v.tensor_tensor(tmp1[:], z1t[:], z1t[:], ALU.mult)
                    v.reduce_sum(stats[:, 1:2], tmp1[:], AX.X)
                    v.tensor_scalar(stats[:, 2:3], stats[:, 0:1], INVD, None,
                                    ALU.mult)
                    v.tensor_tensor(stats[:, 3:4], stats[:, 2:3],
                                    stats[:, 2:3], ALU.mult)
                    v.scalar_tensor_tensor(stats[:, 4:5], stats[:, 1:2], INVD,
                                           stats[:, 3:4], ALU.mult,
                                           ALU.subtract)
                    v.tensor_scalar(stats[:, 4:5], stats[:, 4:5], 1.0, EPS,
                                    ALU.mult, ALU.add).then_inc(sm["z1"], 1)
                    v.wait_ge(sm["sqrt1"], sc + 1)
                    v.reciprocal(stats[:, 6:7], stats[:, 5:6])
                    for _i in range(2):
                        v.tensor_tensor(stats[:, 7:8], stats[:, 6:7],
                                        stats[:, 6:7], ALU.mult)
                        v.tensor_tensor(stats[:, 7:8], stats[:, 7:8],
                                        stats[:, 4:5], ALU.mult)
                        v.tensor_scalar(stats[:, 7:8], stats[:, 7:8], -0.5,
                                        1.5, ALU.mult, ALU.add)
                        v.tensor_tensor(stats[:, 6:7], stats[:, 6:7],
                                        stats[:, 7:8], ALU.mult)
                    if sc >= 2:
                        v.wait_ge(sm["x1sp0" if sc % 2 == 0 else "x1sp1"],
                                  16 * (sc // 2))
                    v.tensor_scalar(x1t[:, sc % 2, :], z1t[:], stats[:, 2:3],
                                    stats[:, 6:7], ALU.subtract, ALU.mult
                                    ).then_inc(sm["x1q"], 1)
                    v.reduce_max(stats[:, 8:9], x1t[:, sc % 2, :], AX.X,
                                 apply_absolute_value=True)
                    v.tensor_scalar(stats[:, 9:10], stats[:, 8:9], 1e-5, None,
                                    ALU.max)
                    v.reciprocal(stats[:, 10:11], stats[:, 9:10])
                    v.tensor_scalar(stats[:, 11:12], stats[:, 10:11], 127.0,
                                    None, ALU.mult)
                    v.tensor_scalar(aqs[:, sc:sc + 1], stats[:, 9:10],
                                    INV127_LO, None, ALU.mult)
                    v.scalar_tensor_tensor(aqs[:, sc:sc + 1], stats[:, 9:10],
                                           INV127_HI, aqs[:, sc:sc + 1],
                                           ALU.mult, ALU.add)
                    v.tensor_scalar(tmp1[:], x1t[:, sc % 2, :],
                                    stats[:, 11:12], 128.0, ALU.mult, ALU.add)
                    v.tensor_copy(tiq[:], tmp1[:])
                    v.scalar_tensor_tensor(c1q[:], tmp1[:], -0.5, tiq[:],
                                           ALU.add, ALU.is_ge)
                    if sc >= 2:
                        v.wait_ge(sm["xtr"], sc - 1)
                    v.scalar_tensor_tensor(xqdq[:, sc % 2, :], tiq[:], -128.0,
                                           c1q[:], ALU.add, ALU.add
                                           ).then_inc(sm["xqd"], 1)
                    if sc >= 16:
                        w2_chunks(6)
            w2_chunks(len(wsched))

        @blk.scalar
        def _(s):
            s = _AD(s)
            def _xtre_head(s, sc):
                j = sc % 4
                s.wait_ge(sm["xtr"], sc + 1)
                if sc >= 4 and j == 0:
                    qbp = sc // 4 - 1
                    s.wait_ge(sm["xqsp0" if qbp % 2 == 0 else "xqsp1"],
                              16 * NHC * (qbp // 2 + 1))
                s.copy(xqs[:, :, j * 128:(j + 1) * 128],
                       ps_tr[:].rearrange("p (a b) -> p a b", a=8)
                       ).then_inc(sm["xtre"], 1)
            for sb in range(NQB):
                s.wait_ge(sm["qkv"], (sb + 1) * 3)
                s.copy(qt[:, sb * 512:(sb + 1) * 512], ps_s[:, 0, :])
                s.copy(kt[:, sb * 512:(sb + 1) * 512], ps_s[:, 1, :])
                s.copy(vt[:, sb * 512:(sb + 1) * 512], ps_h[:]
                       ).then_inc(sm["qkve"], 1)
            for gidx in range(8):
                s.wait_ge(sm["vtr"], gidx + 1)
                s.copy(vtok[:, gidx * 4:(gidx + 1) * 4, :],
                       ps_vt[:].rearrange("p (a b) -> p a b", a=4)
                       ).then_inc(sm["vtre"], 1)
            for qb in range(NQB):
                base = qb * 32
                for kc in range(32):
                    s.wait_ge(sm["ss"], base + kc + 1)
                    if kc >= 2:
                        s.wait_ge(sm["avrs"], base + kc - 1)
                    s.activation(pbuf[:, kc % 2, :], ps_s[:, kc % 2, :],
                                 ACTF.Exp, bias=0.0, scale=ATT_SCALE
                                 ).then_inc(sm["pexp"], 1)
                    if qb >= 1 and kc == 1:
                        _xtre_head(s, qb * 4 - 2)
                        _xtre_head(s, qb * 4 - 1)
                s.wait_ge(sm["avrs"], (qb + 1) * 32)
                s.copy(htb[:], ps_h[:]).then_inc(sm["hev"], 1)
                def _xtre(s, sc):
                    j = sc % 4
                    s.wait_ge(sm["xtr"], sc + 1)
                    if sc >= 4 and j == 0:
                        qbp = sc // 4 - 1
                        s.wait_ge(sm["xqsp0" if qbp % 2 == 0 else "xqsp1"],
                                  16 * NHC * (qbp // 2 + 1))
                    s.copy(xqs[:, :, j * 128:(j + 1) * 128],
                           ps_tr[:].rearrange("p (a b) -> p a b", a=8)
                           ).then_inc(sm["xtre"], 1)
                for j in range(4):
                    sc = qb * 4 + j
                    s.wait_ge(sm["z1"], 2 * (sc + 1))
                    s.activation(stats[:, 5:6], stats[:, 4:5], ACTF.Sqrt,
                                 bias=0.0, scale=1.0).then_inc(sm["sqrt1"], 1)
                    if j >= 2:
                        _xtre(s, sc - 2)
                if qb == NQB - 1:
                    _xtre(s, qb * 4 + 2)
                    _xtre(s, qb * 4 + 3)

        @blk.tensor
        def _(t):
            t.wait_ge(sm["idn"], 3)
            t.wait_ge(sm["ldwb"], 16 * 25)
            for sb in range(NQB):
                t.wait_ge(sm["ldx"], 16 * (sb + 1))
                if sb >= 1:
                    t.wait_ge(sm["qkve"], sb)
                for h in range(NHC):
                    t.matmul(ps_s[:, 0, :], wqkvb[:, h, 0:DH],
                             xts[:, sb % 2, h, :],
                             start=(h == 0), stop=(h == NHC - 1))
                t.drain().then_inc(sm["qkv"], 1)
                for h in range(NHC):
                    t.matmul(ps_s[:, 1, :], wqkvb[:, h, DH:2 * DH],
                             xts[:, sb % 2, h, :],
                             start=(h == 0), stop=(h == NHC - 1))
                t.drain().then_inc(sm["qkv"], 1)
                for h in range(NHC):
                    t.matmul(ps_h[:], wqkvb[:, h, 2 * DH:3 * DH],
                             xts[:, sb % 2, h, :],
                             start=(h == 0), stop=(h == NHC - 1))
                t.drain().then_inc(sm["qkv"], 1)
            for gidx in range(8):
                t.wait_ge(sm["qkve"], gidx + 1)
                if gidx >= 1:
                    t.wait_ge(sm["vtre"], gidx)
                for k in range(4):
                    scv = gidx * 4 + k
                    t.transpose(ps_vt[:, k * 128:(k + 1) * 128],
                                vt[:, scv * 128:(scv + 1) * 128], identr[:])
                t.drain().then_inc(sm["vtr"], 1)
            t.wait_ge(sm["vtre"], 4)
            t.wait_ge(sm["qkve"], 8)
            for qb in range(NQB):
                base = qb * 32
                qsl = slice(qb * 512, (qb + 1) * 512)
                if qb == 3:
                    t.wait_ge(sm["wabs"], n_wch + 1)
                    t.matmul(pv32[0:1, 0:4], onesf[:], wacc[:],
                             start=True, stop=True)
                    t.drain().then_inc(sm["wsum"], 1)
                if qb == 4:
                    t.wait_ge(sm["wscal"], 1)
                    t.matmul(pv32[:, 4:12], ones1f[:], srow[:],
                             start=True, stop=True)
                    t.drain().then_inc(sm["wbc"], 1)
                if qb >= 1:
                    t.wait_ge(sm["hev"], qb)
                    t.wait_ge(sm["rdone"], qb)
                    t.wait_ge(sm["pexp"], base)
                for kc in range(32):
                    if kc >= 2:
                        t.wait_ge(sm["pexp"], base + kc - 1)
                    t.matmul(ps_s[:, kc % 2, :], kt[:, kc * 128:(kc + 1) * 128],
                             qt[:, qsl], start=True, stop=True
                             ).then_inc(sm["ss"], 1)
                    if qb >= 1 and kc == 1:
                        sc3 = qb * 4 - 1
                        t.wait_ge(sm["xqd"], sc3 + 1)
                        t.wait_ge(sm["xtre"], sc3)
                        for ic in range(NHC):
                            t.transpose(ps_tr[:, ic * 128:(ic + 1) * 128],
                                        xqdq[:, sc3 % 2,
                                             ic * 128:(ic + 1) * 128],
                                        identb[:])
                        t.drain().then_inc(sm["xtr"], 1)
                    t.wait_ge(sm["pexp"], base + kc + 1)
                    t.matmul(ps_h[:], vtok[:, kc, :], pbuf[:, kc % 2, :],
                             start=(kc == 0), stop=(kc == 31))
                    t.matmul(ps_r[:], ones_r[:], pbuf[:, kc % 2, :],
                             start=(kc == 0), stop=(kc == 31)
                             ).then_inc(sm["avrs"], 1)
                t.wait_ge(sm["rdone"], qb + 1)
                if qb >= 1:
                    t.wait_ge(sm["z1"], 8 * qb)
                for j in range(4):
                    t.transpose(ps_y1[:, 0, j:j + 1],
                                rinv[0:1, j * 128:(j + 1) * 128],
                                identf[0:1, 0:1])
                t.drain().then_inc(sm["rcolT"], 1)
                t.wait_ge(sm["hev"], qb + 1)
                t.wait_ge(sm["rcole"], qb + 1)
                def _xtr(t, sc):
                    t.wait_ge(sm["xqd"], sc + 1)
                    if sc >= 1:
                        t.wait_ge(sm["xtre"], sc)
                    for ic in range(NHC):
                        t.transpose(ps_tr[:, ic * 128:(ic + 1) * 128],
                                    xqdq[:, sc % 2, ic * 128:(ic + 1) * 128],
                                    identb[:])
                    t.drain().then_inc(sm["xtr"], 1)
                for j in range(4):
                    sc = qb * 4 + j
                    if sc >= 1:
                        t.wait_ge(sm["z1"], 2 * sc)
                    t.matmul(ps_y1[:, 0, :], htb[:, j * 128:(j + 1) * 128],
                             wob[:, 0:512], start=True, stop=True)
                    t.matmul(ps_y1[:, 1, :], htb[:, j * 128:(j + 1) * 128],
                             wob[:, 512:1024], start=True, stop=True
                             ).then_inc(sm["y1"], 1)
                    if j >= 1:
                        _xtr(t, qb * 4 + j - 1)
                if qb == NQB - 1:
                    _xtr(t, qb * 4 + 3)


def _phase_C1(nc, sm, d, pers):
    identb = pers["identb"]; yscale = pers["yscale"]
    wsv = pers["wsv"]; stats = pers["stats"]; aqs = pers["aqs"]

    es = contextlib.ExitStack()
    def SB(name, shape, dt):
        return es.enter_context(nc.sbuf_tensor(name, shape, dt))
    def PS(name, shape, dt):
        return es.enter_context(nc.psum_tensor(name, shape, dt))
    with es:
        xqr = SB("xqr", [P, 2, NHC, 512], BF16)
        wgu = SB("wgu", [P, 2, 2, NHC, 512], BF16)
        hp = SB("hp", [P, SCQ, FF], F32)
        sgt = SB("sgt", [P, 2, 512], F32)
        gtr = SB("gtr", [P, 2, 512], F32)
        scv = SB("scv", [P, 8], F32)
        maxh = SB("maxh", [P, SCQ], F32)
        hq = SB("hq", [P, 2, FF], BF16)
        uq2 = SB("uq2", [P, 1024], F32)
        tiq2 = SB("tiq2", [P, 1024], I32)
        c1q2 = SB("c1q2", [P, 1024], F32)
        hqts = SB("hqts", [P, 2, NOC, DH], BF16)
        ps_g = PS("ps_g", [P, 2, 512], F32)
        ps_u = PS("ps_u", [P, 2, 512], F32)
        ps_t2 = PS("ps_t2", [P, 1024], BF16)
        blk = es.enter_context(nc.Block())
        @blk.sync
        def _(sy):
            sy.wait_ge(sm["wbc"], 1)
            sy.wait_ge(sm["xtre"], NSC)
            sy.wait_ge(sm["ldo0"], 16 * (NSC // 2))
            sy.wait_ge(sm["ldo1"], 16 * (NSC // 2))
            sy.wait_ge(sm["x1sp0"], 16 * (NSC // 2))
            sy.wait_ge(sm["x1sp1"], 16 * (NSC // 2))
            sy.wait_ge(sm["xqsp0"], 16 * NHC * (NQB // 2))
            sy.wait_ge(sm["xqsp1"], 16 * NHC * (NQB // 2))
            sy.wait_ge(sm["ldx"], 16 * NQB)
            for sb in range(NSB):
                if sb >= 2:
                    sy.wait_ge(sm["gu"], (sb - 1) * NOB * SCQ)
                for ic in range(NHC):
                    sy.dma_start(out=xqr[:, sb % 2, ic, :],
                                 in_=d["xqTs"][ic * P:(ic + 1) * P,
                                               sb * 512:(sb + 1) * 512]
                                 ).then_inc(sm["ldxq0" if sb % 2 == 0
                                            else "ldxq1"], 16)
                for ob in range(NOB):
                    idx = sb * NOB + ob
                    if sb == 0:
                        grp = ob // 2
                        need = 16 * (grp + 1)
                        sy.wait_ge(sm["wqsp0"], 16 * (need // 2))
                        sy.wait_ge(sm["wqsp1"], 16 * (need // 2))
                    if idx >= 2:
                        sy.wait_ge(sm["gu"], (idx - 1) * SCQ)
                    for w_i, wsrc in enumerate((d["wgqs"], d["wuqs"])):
                        sy.dma_start(
                            out=wgu[:, idx % 2, w_i, :, :],
                            in_=wsrc.rearrange("(a p) f -> p a f", p=P)[
                                :, :, ob * 512:(ob + 1) * 512]
                        ).then_inc(sm["ldgu0" if idx % 2 == 0
                                   else "ldgu1"], 16)

        @blk.gpsimd
        def _(gp):
            gp = _AD(gp)
            for gsc in range(NSC):
                gp.wait_ge(sm["htre"], (gsc + 1) * 4)
                gp.dma_start(
                    out=d["hqTs"].rearrange("(a p) s -> p a s", p=P)[
                        :, :, gsc * 128:(gsc + 1) * 128],
                    in_=hqts[:, gsc % 2, :, :]).then_inc(sm["hqspill"], 16)
                gp.wait_ge(sm["hqspill"], 16 * (gsc + 1))

        @blk.vector
        def _(v):
            v = _AD(v)
            for sb in range(NSB):
                for sc in range(SCQ):
                    gsc = sb * SCQ + sc
                    v.memset(maxh[:, sc:sc + 1], 0.0)
                    v.tensor_scalar(scv[:, 2 * sc:2 * sc + 1],
                                    aqs[:, gsc:gsc + 1], wsv[:, 0:1], None,
                                    ALU.mult)
                    v.tensor_scalar(scv[:, 2 * sc + 1:2 * sc + 2],
                                    aqs[:, gsc:gsc + 1], wsv[:, 1:2], None,
                                    ALU.mult)
                for ob in range(NOB):
                    for sc in range(SCQ):
                        k = (sb * NOB + ob) * SCQ + sc
                        v.wait_ge(sm["gu"], k + 1)
                        if k >= 2:
                            v.wait_ge(sm["silu"], k - 1)
                        v.tensor_scalar(gtr[:, sc % 2, :], ps_g[:, sc % 2, :],
                                        scv[:, 2 * sc:2 * sc + 1], None,
                                        ALU.mult).then_inc(sm["gpre"], 1)
                        v.wait_ge(sm["silu"], k + 1)
                        v.tensor_tensor(hp[:, sc, ob * 512:(ob + 1) * 512],
                                        sgt[:, sc % 2, :], ps_u[:, sc % 2, :],
                                        ALU.mult)
                        v.reduce_max(stats[:, 15:16],
                                     hp[:, sc, ob * 512:(ob + 1) * 512], AX.X,
                                     apply_absolute_value=True)
                        v.tensor_tensor(maxh[:, sc:sc + 1], maxh[:, sc:sc + 1],
                                        stats[:, 15:16], ALU.max
                                        ).then_inc(sm["mid"], 1)
                for sc in range(SCQ):
                    gsc = sb * SCQ + sc
                    v.tensor_scalar(stats[:, 0:1], maxh[:, sc:sc + 1],
                                    scv[:, 2 * sc + 1:2 * sc + 2], 1e-5,
                                    ALU.mult, ALU.max)
                    v.reciprocal(stats[:, 1:2], stats[:, 0:1])
                    v.scalar_tensor_tensor(stats[:, 2:3], stats[:, 1:2], 127.0,
                                           scv[:, 2 * sc + 1:2 * sc + 2],
                                           ALU.mult, ALU.mult)
                    v.tensor_scalar(stats[:, 3:4], stats[:, 0:1],
                                    INV127_LO, None, ALU.mult)
                    v.scalar_tensor_tensor(stats[:, 3:4], stats[:, 0:1],
                                           INV127_HI, stats[:, 3:4],
                                           ALU.mult, ALU.add)
                    v.tensor_scalar(yscale[:, gsc:gsc + 1], stats[:, 3:4],
                                    wsv[:, 2:3], None, ALU.mult)
                    if gsc >= 2:
                        v.wait_ge(sm["htr"], (gsc - 1) * 4)
                    for hf in range(4):
                        hsl = slice(hf * 1024, (hf + 1) * 1024)
                        v.tensor_scalar(uq2[:], hp[:, sc, hsl], stats[:, 2:3],
                                        128.0, ALU.mult, ALU.add)
                        v.tensor_copy(tiq2[:], uq2[:])
                        v.scalar_tensor_tensor(c1q2[:], uq2[:], -0.5, tiq2[:],
                                               ALU.add, ALU.is_ge)
                        v.scalar_tensor_tensor(hq[:, gsc % 2, hsl], tiq2[:],
                                               -128.0, c1q2[:], ALU.add,
                                               ALU.add)
                    v.drain().then_inc(sm["rnd"], 1)

        @blk.scalar
        def _(s):
            s = _AD(s)
            for sb in range(NSB):
                for ob in range(NOB):
                    for sc in range(SCQ):
                        k = (sb * NOB + ob) * SCQ + sc
                        s.wait_ge(sm["gpre"], k + 1)
                        if k >= 2:
                            s.wait_ge(sm["mid"], k - 1)
                        s.activation(sgt[:, sc % 2, :], gtr[:, sc % 2, :],
                                     ACTF.Silu, bias=0.0, scale=1.0
                                     ).then_inc(sm["silu"], 1)
                for sc in range(SCQ):
                    gsc = sb * SCQ + sc
                    for gi in range(4):
                        s.wait_ge(sm["htr"], gsc * 4 + gi + 1)
                        if gsc >= 2 and gi == 0:
                            s.wait_ge(sm["hqspill"], 16 * (gsc - 1))
                        s.copy(hqts[:, gsc % 2, gi * 8:(gi + 1) * 8, :],
                               ps_t2[:].rearrange("p (a b) -> p a b", a=8)
                               ).then_inc(sm["htre"], 1)

        @blk.tensor
        def _(t):
            for sb in range(NSB):
                t.wait_ge(sm["ldxq0" if sb % 2 == 0 else "ldxq1"],
                          16 * NHC * (sb // 2 + 1))
                for ob in range(NOB):
                    idx = sb * NOB + ob
                    t.wait_ge(sm["ldgu0" if idx % 2 == 0 else "ldgu1"],
                              16 * 2 * (idx // 2 + 1))
                    for sc in range(SCQ):
                        k = idx * SCQ + sc
                        if k >= 2:
                            t.wait_ge(sm["gpre"], k - 1)
                            t.wait_ge(sm["mid"], k - 1)
                        for ic in range(NHC):
                            t.matmul(ps_g[:, sc % 2, :],
                                     xqr[:, sb % 2, ic,
                                         sc * 128:(sc + 1) * 128],
                                     wgu[:, idx % 2, 0, ic, :],
                                     start=(ic == 0), stop=(ic == NHC - 1))
                        for ic in range(NHC):
                            t.matmul(ps_u[:, sc % 2, :],
                                     xqr[:, sb % 2, ic,
                                         sc * 128:(sc + 1) * 128],
                                     wgu[:, idx % 2, 1, ic, :],
                                     start=(ic == 0), stop=(ic == NHC - 1))
                        t.drain().then_inc(sm["gu"], 1)
                for sc in range(SCQ):
                    gsc = sb * SCQ + sc
                    t.wait_ge(sm["rnd"], gsc + 1)
                    for gi in range(4):
                        if gsc * 4 + gi >= 1:
                            t.wait_ge(sm["htre"], gsc * 4 + gi)
                        for k in range(8):
                            oc = gi * 8 + k
                            t.transpose(ps_t2[:, k * 128:(k + 1) * 128],
                                        hq[:, gsc % 2,
                                           oc * 128:(oc + 1) * 128],
                                        identb[:])
                        t.drain().then_inc(sm["htr"], 1)


def _phase_C2(nc, sm, d, pers):
    yscale = pers["yscale"]; stats = pers["stats"]

    es = contextlib.ExitStack()
    def SB(name, shape, dt):
        return es.enter_context(nc.sbuf_tensor(name, shape, dt))
    def PS(name, shape, dt):
        return es.enter_context(nc.psum_tensor(name, shape, dt))
    with es:
        wdq = SB("wdq", [P, NOC, D], BF16)
        hqr = SB("hqr", [P, 4, NOC, 2 * DH], BF16)
        x1r = SB("x1r", [P, 2, D], F32)
        z2t = SB("z2t", [P, D], F32)
        sq2 = SB("sq2", [P, D], F32)
        outt = SB("outt", [P, 2, D], F32)
        ps_y2 = PS("ps_y2", [P, 3, 2, 512], F32)
        blk = es.enter_context(nc.Block())
        @blk.sync
        def _(sy):
            sy.wait_ge(sm["wqsp0"], 16 * 48)
            sy.wait_ge(sm["wqsp1"], 16 * 48)
            sy.wait_ge(sm["gu"], NSB * NOB * SCQ)
            sy.wait_ge(sm["rnd"], NSC)
            sy.wait_ge(sm["ldxq0"], 16 * NHC * (NSB // 2))
            sy.wait_ge(sm["ldxq1"], 16 * NHC * (NSB // 2))
            sy.wait_ge(sm["ldgu0"], 16 * 2 * (NSB * NOB // 2))
            sy.wait_ge(sm["ldgu1"], 16 * 2 * (NSB * NOB // 2))
            def ld_pair(pr):
                sc = 2 * pr
                sy.wait_ge(sm["hqspill"], 16 * (sc + 2))
                if sc >= 8:
                    sy.wait_ge(sm["dwn"], sc - 6)
                for half in range(2):
                    sy.dma_start(
                        out=hqr[:, pr % 4, half * 16:(half + 1) * 16, :],
                        in_=d["hqTs"].rearrange("(a p) s -> p a s", p=P)[
                            :, half * 16:(half + 1) * 16,
                            sc * 128:(sc + 2) * 128]
                    ).then_inc(sm["ldhq%d" % (pr % 4)], 16)
            ld_pair(0)
            ld_pair(1)
            for oc in range(NOC):
                sy.dma_start(out=wdq[:, oc, :],
                             in_=d["wdqs"][oc * P:(oc + 1) * P, :]
                             ).then_inc(sm["ldwd"], 16)
                if oc % 8 == 7:
                    sy.wait_ge(sm["ldwd"], 16 * (oc + 1))
            for sc in range(4, NSC, 2):
                ld_pair(sc // 2)
            sy.wait_ge(sm["outd0"], 16 * (NSC // 2))
            sy.wait_ge(sm["outd1"], 16 * (NSC // 2))

        @blk.gpsimd
        def _(gp):
            gp = _AD(gp)
            def out_dma(sc):
                gp.wait_ge(sm["outv"], sc + 1)
                gp.dma_start(out=d["out"][sc * P:(sc + 1) * P, :],
                             in_=outt[:, sc % 2, :]
                             ).then_inc(sm["outd0" if sc % 2 == 0
                                        else "outd1"], 16)
            for sc in range(NSC):
                if sc >= 2:
                    gp.wait_ge(sm["z2st"], 2 * (sc - 1))
                gp.dma_start(out=x1r[:, sc % 2, :],
                             in_=d["x1s"][sc * P:(sc + 1) * P, :]
                             ).then_inc(sm["ldx10" if sc % 2 == 0
                                        else "ldx11"], 16)
                if sc >= 2:
                    out_dma(sc - 2)
            out_dma(NSC - 2)
            out_dma(NSC - 1)

        @blk.vector
        def _(v):
            v = _AD(v)
            for sc in range(NSC):
                v.wait_ge(sm["dwn"], sc + 1)
                v.wait_ge(sm["ldx10" if sc % 2 == 0 else "ldx11"],
                          16 * (sc // 2 + 1))
                v.scalar_tensor_tensor(z2t[:, 0:512], ps_y2[:, sc % 3, 0, :],
                                       yscale[:, sc:sc + 1],
                                       x1r[:, sc % 2, 0:512],
                                       ALU.mult, ALU.add)
                v.scalar_tensor_tensor(z2t[:, 512:1024], ps_y2[:, sc % 3, 1, :],
                                       yscale[:, sc:sc + 1],
                                       x1r[:, sc % 2, 512:1024],
                                       ALU.mult, ALU.add).then_inc(sm["z2st"], 1)
                v.reduce_sum(stats[:, 0:1], z2t[:], AX.X)
                v.tensor_tensor(sq2[:], z2t[:], z2t[:], ALU.mult)
                v.reduce_sum(stats[:, 1:2], sq2[:], AX.X)
                v.tensor_scalar(stats[:, 2:3], stats[:, 0:1], INVD, None,
                                ALU.mult)
                v.tensor_tensor(stats[:, 3:4], stats[:, 2:3], stats[:, 2:3],
                                ALU.mult)
                v.scalar_tensor_tensor(stats[:, 4:5], stats[:, 1:2], INVD,
                                       stats[:, 3:4], ALU.mult, ALU.subtract)
                v.tensor_scalar(stats[:, 4:5], stats[:, 4:5], 1.0, EPS,
                                ALU.mult, ALU.add).then_inc(sm["z2st"], 1)
                v.wait_ge(sm["sqrt2"], sc + 1)
                v.reciprocal(stats[:, 6:7], stats[:, 5:6])
                for _i in range(2):
                    v.tensor_tensor(stats[:, 7:8], stats[:, 6:7],
                                    stats[:, 6:7], ALU.mult)
                    v.tensor_tensor(stats[:, 7:8], stats[:, 7:8],
                                    stats[:, 4:5], ALU.mult)
                    v.tensor_scalar(stats[:, 7:8], stats[:, 7:8], -0.5, 1.5,
                                    ALU.mult, ALU.add)
                    v.tensor_tensor(stats[:, 6:7], stats[:, 6:7],
                                    stats[:, 7:8], ALU.mult)
                if sc >= 2:
                    v.wait_ge(sm["outd0" if sc % 2 == 0 else "outd1"],
                              16 * (sc // 2))
                v.tensor_scalar(outt[:, sc % 2, :], z2t[:], stats[:, 2:3],
                                stats[:, 6:7], ALU.subtract, ALU.mult
                                ).then_inc(sm["outv"], 1)

        @blk.scalar
        def _(s):
            s = _AD(s)
            for sc in range(NSC):
                s.wait_ge(sm["z2st"], 2 * (sc + 1))
                s.activation(stats[:, 5:6], stats[:, 4:5], ACTF.Sqrt,
                             bias=0.0, scale=1.0).then_inc(sm["sqrt2"], 1)

        @blk.tensor
        def _(t):
            for sc in range(NSC):
                pr = sc // 2
                t.wait_ge(sm["ldhq%d" % (pr % 4)], 16 * 2 * (pr // 4 + 1))
                if sc >= 3:
                    t.wait_ge(sm["z2st"], 2 * (sc - 2))
                for hf in range(2):
                    for oc in range(NOC):
                        if sc == 0 and hf == 0 and oc % 8 == 0:
                            t.wait_ge(sm["ldwd"], 16 * (oc + 8))
                        t.matmul(ps_y2[:, sc % 3, hf, :],
                                 hqr[:, pr % 4, oc,
                                     (sc % 2) * DH:(sc % 2 + 1) * DH],
                                 wdq[:, oc, hf * 512:(hf + 1) * 512],
                                 start=(oc == 0), stop=(oc == NOC - 1))
                t.drain().then_inc(sm["dwn"], 1)


def kernel(L0_attn_x, L0_attn_head_dim_mask, L0_attn_w_q, L0_attn_w_k,
           L0_attn_w_v, L0_attn_w_o, L0_ffn_w_gate, L0_ffn_w_up,
           L0_ffn_w_down):
    x = np.ascontiguousarray(np.asarray(L0_attn_x, dtype=np.float32))
    B = x.shape[0]
    xT = np.ascontiguousarray(x.transpose(0, 2, 1))
    mask = np.ascontiguousarray(
        np.asarray(L0_attn_head_dim_mask, dtype=np.int32).reshape(DH, 1))
    w_q = np.ascontiguousarray(np.asarray(L0_attn_w_q, dtype=np.float32))
    w_k = np.ascontiguousarray(np.asarray(L0_attn_w_k, dtype=np.float32))
    w_v = np.ascontiguousarray(np.asarray(L0_attn_w_v, dtype=np.float32))
    w_o = np.ascontiguousarray(np.asarray(L0_attn_w_o, dtype=np.float32))
    wgT = np.ascontiguousarray(np.asarray(L0_ffn_w_gate, dtype=np.float32).T)
    wuT = np.ascontiguousarray(np.asarray(L0_ffn_w_up, dtype=np.float32).T)
    wdT = np.ascontiguousarray(np.asarray(L0_ffn_w_down, dtype=np.float32).T)

    nc = build_program()
    in_maps = [{
        "xT": xT[b], "x": x[b], "w_q": w_q, "w_k": w_k, "w_v": w_v,
        "w_o": w_o, "mask": mask, "wgT": wgT, "wuT": wuT, "wdT": wdT,
    } for b in range(B)]
    res = run_bass_kernel_spmd(nc, in_maps, core_ids=list(range(B)))
    out = np.stack([np.asarray(r["out"]) for r in res.results], axis=0)
    return out if out.dtype == np.float64 else out.astype(np.float32)
